# version 04
# baseline (speedup 1.0000x reference)
import contextlib
import numpy as np
import ml_dtypes
import concourse.bass as bass
import concourse.mybir as mybir
from concourse.bass_utils import run_bass_kernel_spmd

F32 = mybir.dt.float32
BF16 = mybir.dt.bfloat16
I32 = mybir.dt.int32
ALU = mybir.AluOpType
AF = mybir.ActivationFunctionType
AX = mybir.AxisListType
BF = ml_dtypes.bfloat16

T = 4096
D = 1024
NT = 32
NG = 8
C = 8
H = 16
HD = 64
HID = 3072
HC = 24
EPS = 1e-6
LNEPS = 1e-5
QSC = 1.0 / 127.0
DSC = 1.0 / (127.0 * 127.0)
SFL = 4096.0

KiB = 1024
ARENA = 207 * KiB


def build_program(debug=False, upto=6):
    nc = bass.Bass("TRN2", target_bir_lowering=False,
                   detect_race_conditions=False)

    x_d = nc.dram_tensor("x", [T, D], F32, kind="ExternalInput").ap()
    wq_d = nc.dram_tensor("wqT", [D, D], F32, kind="ExternalInput").ap()
    wk_d = nc.dram_tensor("wkT", [D, D], F32, kind="ExternalInput").ap()
    wv_d = nc.dram_tensor("wvT", [D, D], F32, kind="ExternalInput").ap()
    wo_d = nc.dram_tensor("woT", [D, D], F32, kind="ExternalInput").ap()
    b_d = nc.dram_tensor("bcat", [1, 4 * D], F32, kind="ExternalInput").ap()
    wg_d = nc.dram_tensor("wgT", [D, HID], I32, kind="ExternalInput").ap()
    wu_d = nc.dram_tensor("wuT", [D, HID], I32, kind="ExternalInput").ap()
    wd_d = nc.dram_tensor("wdT", [HID, D], I32, kind="ExternalInput").ap()
    out_d = nc.dram_tensor("out", [T, D], F32, kind="ExternalOutput").ap()

    skind = "ExternalOutput" if debug else "Internal"
    q_sc = nc.dram_tensor("q_sc", [T, D], BF16, kind=skind).ap()
    k_sc = nc.dram_tensor("k_sc", [T, D], BF16, kind=skind).ap()
    v_sc = nc.dram_tensor("v_sc", [T, D], BF16, kind=skind).ap()
    ln2_sc = nc.dram_tensor("ln2_sc", [T, D], BF16, kind=skind).ap()
    if debug:
        dbg_si = nc.dram_tensor("dbg_si", [128, NT, H], F32, kind="ExternalOutput").ap()
        dbg_sia = nc.dram_tensor("dbg_sia", [128, NT, H], BF16, kind="ExternalOutput").ap()
        dbg_sc = nc.dram_tensor("dbg_sc", [128, NT, H], BF16, kind="ExternalOutput").ap()
        dbg_kv = nc.dram_tensor("dbg_kv", [128, C, 128], BF16, kind="ExternalOutput").ap()
        dbg_rows = nc.dram_tensor("dbg_rows", [1, 4 * D], BF16, kind="ExternalOutput").ap()
        dbg_ln1 = nc.dram_tensor("dbg_ln1", [128, D], BF16, kind="ExternalOutput").ap()
        dbg_st = nc.dram_tensor("dbg_st", [128, 8], F32, kind="ExternalOutput").ap()
        dbg_lnh = nc.dram_tensor("dbg_lnh", [128, C, 128], BF16, kind="ExternalOutput").ap()

    ctx = contextlib.ExitStack()
    with ctx:
        arena_h = ctx.enter_context(
            nc.sbuf_tensor("arena", [128, ARENA], mybir.dt.uint8))
        arena_base = nc.lookup_mloc(arena_h).addr

        def place(name, shape, dtype, off):
            assert off + int(np.prod(shape[1:])) * mybir.dt.size(dtype) <= ARENA, name
            nc.dbg_offsets = getattr(nc, "dbg_offsets", {})
            nc.dbg_offsets[name] = (off, tuple(shape), dtype)
            h = nc.virtual_sbuf_tensor(name, shape, dtype)
            nc._tensor(h.name, h.shape, h.dtype, type="SB")
            mloc = nc.lookup_mloc(h)
            mloc.allocated = True
            mloc.addr = arena_base + off
            return h

        o = [0]
        def perm(name, shape, dtype):
            h = place(name, shape, dtype, o[0])
            o[0] += (int(np.prod(shape[1:])) * mybir.dt.size(dtype) + 31) // 32 * 32
            return h

        ones_r = perm("ones_r", [1, 512], BF16)
        ones_c = perm("ones_c", [128, 1], BF16)
        identb = perm("identb", [128, 128], BF16)
        b_sb = perm("b_sb", [1, 4 * D], BF16)
        ks_row = perm("ks_row", [1, D], BF16)
        qs_row = perm("qs_row", [1, D], BF16)
        sk_row = perm("sk_row", [1, D], BF16)
        sq_row = perm("sq_row", [1, D], BF16)
        ksb = perm("ksb", [128, C, 2], BF16)
        qsb = perm("qsb", [128, C, 2], BF16)
        skb = perm("skb", [128, C, 2], BF16)
        sqb16 = perm("sqb16", [128, C, H], BF16)
        eks_row = perm("eks_row", [1, H], BF16)
        eqs_row = perm("eqs_row", [1, H], BF16)
        esk_row = perm("esk_row", [1, H], BF16)
        esq_row = perm("esq_row", [1, H], BF16)
        ep_tmp = perm("ep_tmp", [1, H], F32)
        si_all = perm("si_all", [128, NT, H], F32)
        sia_all = perm("sia_all", [128, NT, H], BF16)
        sc_tm = perm("sc_tm", [128, NT, H], BF16)
        kv2 = perm("kv2", [128, C, 128], BF16)
        mx16 = perm("mx16", [16, 1], F32)
        nmx16 = perm("nmx16", [16, 1], F32)
        se16 = perm("se16", [16, 1], F32)
        rse16 = perm("rse16", [16, 1], F32)
        nsum = perm("nsum", [128, 4], F32)
        negmu = perm("negmu", [128, 4], F32)
        ssq = perm("ssq", [128, 4], F32)
        e2c = perm("e2c", [128, 1], F32)
        nvc = perm("nvc", [128, 1], F32)
        vpe = perm("vpe", [128, 4], F32)
        sigc = perm("sigc", [128, 4], F32)
        rsig = perm("rsig", [128, 4], F32)
        sa_t = perm("sa_t", [128, 2, H], F32)
        so_t = perm("so_t", [128, 2, H], F32)
        diag16 = perm("diag16", [16, 16], BF16)
        assert o[0] <= 25 * KiB, o[0]

        h_fm = place("h_fm", [128, HC, 512], BF16, 0)
        wo_sb = place("wo_sb", [128, C, D], BF16, 25 * KiB)
        lnh2 = [place(f"lnh2_{i}", [128, C, 512], BF16, 25 * KiB + i * 8 * KiB)
                for i in range(2)]

        wq_sb = place("wq_sb", [128, C, D], BF16, 41 * KiB)
        wk_sb = place("wk_sb", [128, C, D], BF16, 57 * KiB)
        wv_sb = place("wv_sb", [128, C, D], BF16, 73 * KiB)
        wg_sb = place("wg_sb", [128, C, HID], BF16, 41 * KiB)

        b3 = 89 * KiB
        xt1 = [place(f"xt1_{i}", [128, D], F32, b3 + i * 4 * KiB) for i in range(3)]
        sq_scr = place("sq_scr", [128, D], F32, b3 + 12 * KiB)
        ln1t = [place(f"ln1t_{i}", [128, D], BF16, b3 + 16 * KiB + i * 2 * KiB)
                for i in range(3)]
        lnh1 = [place(f"lnh1_{i}", [128, C, 128], BF16, b3 + 22 * KiB + i * 2 * KiB)
                for i in range(4)]
        qt1 = [place(f"qt1_{i}", [128, D], BF16, b3 + 30 * KiB + i * 2 * KiB)
               for i in range(2)]
        kt1 = [place(f"kt1_{i}", [128, D], BF16, b3 + 34 * KiB + i * 2 * KiB)
               for i in range(2)]
        vt1 = [place(f"vt1_{i}", [128, D], BF16, b3 + 38 * KiB + i * 2 * KiB)
               for i in range(2)]
        wu_sb = place("wu_sb", [128, C, HID], BF16, b3)

        b4 = 137 * KiB
        kfm = [place(f"kfm_{i}", [128, C, 512], BF16, b4 + i * 8 * KiB)
               for i in range(2)]
        csrc_fm = place("csrc_fm", [16, T], F32, b4 + 16 * KiB)
        exp_fm = place("exp_fm", [16, T], BF16, b4 + 32 * KiB)
        qxfm = [place(f"qxfm_{i}", [128, C, 512], BF16, b4 + i * 8 * KiB)
                for i in range(3)]
        x1t = [place(f"x1t_{i}", [128, D], F32, b4 + 24 * KiB + i * 4 * KiB)
               for i in range(2)]
        ln2t = [place(f"ln2t_{i}", [128, D], BF16, b4 + 32 * KiB + i * 2 * KiB)
                for i in range(2)]
        sq_scr2 = place("sq_scr2", [128, D], F32, b4 + 36 * KiB)
        wd_a = place("wd_a", [128, 20, D], BF16, b4)
        wd_b = place("wd_b", [128, 4, D], BF16, b4 + 40 * KiB)

        b5 = 177 * KiB
        qfm = [place(f"qfm_{i}", [128, C, 512], BF16, b5 + i * 8 * KiB)
               for i in range(2)]
        qst = [place(f"qst_{i}", [128, D], BF16, b5 + 16 * KiB + i * 2 * KiB)
               for i in range(2)]
        qsi_t = [place("qsi_0", [128, D], BF16, b5 + 20 * KiB)]
        kso_t = [place("kso_0", [128, D], BF16, b5 + 22 * KiB)]
        kst2 = [place(f"kst2_{i}", [128, D], BF16, b5 + 24 * KiB + i * 2 * KiB)
                for i in range(2)]
        kst = [place(f"kst_{i}", [128, D], BF16, b5 + i * 2 * KiB) for i in range(3)]
        vst = [place(f"vst_{i}", [128, D], BF16, b5 + 6 * KiB + i * 2 * KiB)
               for i in range(3)]
        vsc_t = [place(f"vsc_{i}", [128, D], BF16, b5 + 12 * KiB + i * 2 * KiB)
                 for i in range(2)]
        qst5 = [place(f"qst5_{i}", [128, D], BF16, b5 + 16 * KiB + i * 2 * KiB)
                for i in range(4)]
        qx_t = [place(f"qx_{i}", [128, D], BF16, b5 + 24 * KiB + i * 2 * KiB)
                for i in range(2)]
        xt5 = [place(f"xt5_{i}", [128, D], F32, b5 + i * 4 * KiB)
               for i in range(3)]
        sg_t = [place(f"sg_{i}", [128, 512], BF16, b5 + 8 * KiB + i * 1 * KiB)
                for i in range(2)]
        fout = [place(f"fout_{i}", [128, D], F32, b5 + 10 * KiB + i * 4 * KiB)
                for i in range(2)]
        c127 = place("c127", [128, 1], F32, b5 + 28 * KiB)
        assert b5 + 30 * KiB <= ARENA

        sems_list = []

        def sem(name2):
            h = ctx.enter_context(nc.semaphore(name2))
            sems_list.append(h)
            return h

        def sems(name, n):
            return [sem(f"{name}{i}") for i in range(n)]

        s_w = sem("s_w"); s_wo = sem("s_wo")
        s_x1 = sems("s_x1", 3); s_lnT = sems("s_lnT", 4)
        s_qw = sems("s_qw", 2); s_kw = sems("s_kw", 2); s_vw = sems("s_vw", 2)
        a_sq = sem("a_sq"); a_sig = sem("a_sig")
        d_vpe = sem("d_vpe"); a_sqr = sem("a_sqr"); d_ln1 = sem("d_ln1")
        d_vev = sem("d_vev"); t_q = sem("t_q"); t_k = sem("t_k"); t_v = sem("t_v")
        t_sum = sem("t_sum"); d_rows = sem("d_rows"); t_cols = sem("t_cols")
        d_blk = sem("d_blk"); s_q2 = sems("s_q2", 2); s_k2 = sems("s_k2", 2)
        s_fm2 = sems("s_fm2", 2)
        t_dA = sem("t_dA"); t_dB = sem("t_dB"); d_rec = sem("d_rec")
        d_p2 = sem("d_p2"); t_sum2 = sem("t_sum2"); s_fm3 = sems("s_fm3", 2)
        t_C = sem("t_C"); d_C = sem("d_C"); t_D = sem("t_D"); a_sa = sem("a_sa")
        d_sia = sem("d_sia"); d_mx = sem("d_mx"); a_exp = sem("a_exp")
        d_scf = sem("d_scf"); t_scT = sem("t_scT"); d_sct = sem("d_sct")
        s_k4 = sems("s_k4", 3); s_v4 = sems("s_v4", 3)
        d_vsc = sem("d_vsc"); t_kv4 = sem("t_kv4")
        d_kv2 = sem("d_kv2"); s_q5 = sems("s_q5", 4); s_x5 = sems("s_x5", 3)
        d_qx = sem("d_qx"); t_ko = sem("t_ko")
        t_qxT = sem("t_qxT"); a_qx = sem("a_qx")
        e5a = sem("e5a"); e5d = sem("e5d"); t_wo = sem("t_wo"); d_x1 = sem("d_x1")
        a_sq2 = sem("a_sq2"); d_vp2 = sem("d_vp2"); a_sr2 = sem("a_sr2")
        d_ln2 = sem("d_ln2"); s_out1 = sems("s_out1", 2); s_out2 = sems("s_out2", 2)
        s_ln2w = sems("s_ln2w", 2)
        s_wf = sem("s_wf"); s_wd = sem("s_wd"); s_lh2 = sems("s_lh2", 2)
        t_G = sem("t_G"); t_U = sem("t_U")
        s_cst = sem("s_cst")
        a_sg = sem("a_sg"); d_h = sem("d_h"); t_dn = sem("t_dn"); d_fo = sem("d_fo")
        s_dbg = sem("s_dbg"); d_self = sem("d_self")
        dsc = [0]

        P = [ctx.enter_context(nc.psum_tensor(f"P{i}", [128, 512], F32))
             for i in range(8)]
        P0, P1, P2, P3, P4, P5, P6, P7 = P

        def bfv(ps, cols):
            return ps[:, :].bitcast(BF16)[:, 0:cols]

        block = ctx.enter_context(nc.Block())

        all_sems = [h for h in sems_list]
        nc.all_engine_barrier()
        lo = min(h.num for h in all_sems)
        hi = max(h.num for h in all_sems)
        nc.gpsimd.dma_reset(range(lo, hi + 1))
        nc.gpsimd.sem_clear(range(lo, hi + 1))
        nc.all_engine_barrier()

        @block.gpsimd
        def _(g):
            g.memset(ones_r[:], 1.0).then_inc(s_cst, 1)
            g.memset(ones_c[:], 1.0).then_inc(s_cst, 1)
            g.memset(c127[:], QSC).then_inc(s_cst, 1)
            g.memset(ksb[:], 0.0).then_inc(s_cst, 1)
            g.memset(qsb[:], 0.0).then_inc(s_cst, 1)
            g.memset(skb[:], 0.0).then_inc(s_cst, 1)
            g.memset(sqb16[:], 0.0).then_inc(s_cst, 1)
            g.memset(kv2[:], 0.0).then_inc(s_cst, 1)
            g.memset(identb[:], 0.0).then_inc(s_cst, 1)
            g.wait_ge(s_cst, 9)
            g.affine_select(out=identb[:], in_=identb[:],
                            compare_op=ALU.not_equal, fill=1.0, base=0,
                            pattern=[[-1, 128]], channel_multiplier=1
                            ).then_inc(s_cst, 1)
            g.dma_start(out=wq_sb[:], in_=wq_d.rearrange("(c p) i -> p c i", p=128)
                        ).then_inc(s_w, 16)
            g.dma_start(out=wk_sb[:], in_=wk_d.rearrange("(c p) i -> p c i", p=128)
                        ).then_inc(s_w, 16)
            g.dma_start(out=wv_sb[:], in_=wv_d.rearrange("(c p) i -> p c i", p=128)
                        ).then_inc(s_w, 16)
            g.dma_start(out=b_sb[:], in_=b_d).then_inc(s_w, 16)
            g.dma_start(out=wo_sb[:], in_=wo_d.rearrange("(c p) i -> p c i", p=128)
                        ).then_inc(s_wo, 16)
            for t in range(NT):
                if t >= 3:
                    g.wait_ge(a_sq, 2 * (t - 2))
                    g.wait_ge(d_ln1, t - 2)
                g.dma_start(out=xt1[t % 3][:], in_=x_d[t * 128:(t + 1) * 128, :]
                            ).then_inc(s_x1[t % 3], 16)
                if t >= 1:
                    tt = t - 1
                    g.wait_ge(a_sig, 2 * tt + 1)
                    g.dma_start(out=q_sc[tt * 128:(tt + 1) * 128, :],
                                in_=qt1[tt % 2][:]).then_inc(s_qw[tt % 2], 16)
                    g.wait_ge(a_sig, 2 * tt + 2)
                    g.dma_start(out=k_sc[tt * 128:(tt + 1) * 128, :],
                                in_=kt1[tt % 2][:]).then_inc(s_kw[tt % 2], 16)
                    g.wait_ge(d_vev, tt + 1)
                    g.dma_start(out=v_sc[tt * 128:(tt + 1) * 128, :],
                                in_=vt1[tt % 2][:]).then_inc(s_vw[tt % 2], 16)
            tt = NT - 1
            g.wait_ge(a_sig, 2 * tt + 1)
            g.dma_start(out=q_sc[tt * 128:, :], in_=qt1[tt % 2][:]).then_inc(s_qw[tt % 2], 16)
            g.wait_ge(a_sig, 2 * tt + 2)
            g.dma_start(out=k_sc[tt * 128:, :], in_=kt1[tt % 2][:]).then_inc(s_kw[tt % 2], 16)
            g.wait_ge(d_vev, tt + 1)
            g.dma_start(out=v_sc[tt * 128:, :], in_=vt1[tt % 2][:]).then_inc(s_vw[tt % 2], 16)
            if upto < 2:
                for s_ in (s_qw + s_kw + s_vw):
                    g.wait_ge(s_, 16 * (NT // 2))
                if debug:
                    g.dma_start(out=dbg_ln1, in_=ln1t[2][:]).then_inc(s_dbg, 16)
                    g.dma_start(out=dbg_lnh, in_=lnh1[2][:]).then_inc(s_dbg, 16)
                    g.dma_start(out=dbg_st[:, 0:2], in_=ssq[:], allow_slow_non_contiguous=True).then_inc(s_dbg, 16)
                    g.dma_start(out=dbg_st[:, 2:3], in_=negmu[:], allow_slow_non_contiguous=True).then_inc(s_dbg, 16)
                    g.dma_start(out=dbg_st[:, 3:4], in_=rsig[:], allow_slow_non_contiguous=True).then_inc(s_dbg, 16)
                    g.dma_start(out=dbg_st[:, 4:6], in_=vpe[:], allow_slow_non_contiguous=True).then_inc(s_dbg, 16)
                    g.dma_start(out=dbg_st[:, 6:8], in_=sigc[:], allow_slow_non_contiguous=True).then_inc(s_dbg, 16)
                    g.wait_ge(s_dbg, 16 * 7)
                return
            for t in range(NT):
                if t >= 2:
                    g.wait_ge(d_p2, t - 1)
                g.dma_start(out=qst[t % 2][:], in_=q_sc[t * 128:(t + 1) * 128, :]
                            ).then_inc(s_q2[t % 2], 16)
                g.dma_start(out=kst2[t % 2][:],
                            in_=k_sc[t * 128:(t + 1) * 128, :]).then_inc(s_k2[t % 2], 16)
            if upto < 3:
                return
            g.dma_start(out=wg_sb[:], in_=wg_d.rearrange("(c p) h -> p c h", p=128)
                        ).then_inc(s_wf, 16)
            g.dma_start(out=wu_sb[:], in_=wu_d.rearrange("(c p) h -> p c h", p=128)
                        ).then_inc(s_wf, 16)
            if upto < 4:
                return
            g.wait_ge(t_D, NT)
            g.wait_ge(t_scT, NT)
            g.wait_ge(s_vw[0], 16 * (NT // 2))
            g.wait_ge(s_vw[1], 16 * (NT // 2))
            g.wait_ge(d_p2, NT)
            for t in range(NT):
                if t >= 3:
                    g.wait_ge(t_kv4, t - 2)
                    g.wait_ge(d_vsc, t - 2)
                g.dma_start(out=kst[t % 3][:], in_=k_sc[t * 128:(t + 1) * 128, :]
                            ).then_inc(s_k4[t % 3], 16)
                g.dma_start(out=vst[t % 3][:], in_=v_sc[t * 128:(t + 1) * 128, :]
                            ).then_inc(s_v4[t % 3], 16)
                if t < 4:
                    g.dma_start(out=qst5[t % 4][:],
                                in_=q_sc[t * 128:(t + 1) * 128, :]
                                ).then_inc(s_q5[t % 4], 16)
            if upto < 5:
                return
            for t in range(NT):
                if 4 <= t < NT:
                    g.wait_ge(d_qx, t - 3)
                    g.dma_start(out=qst5[t % 4][:],
                                in_=q_sc[t * 128:(t + 1) * 128, :]
                                ).then_inc(s_q5[t % 4], 16)
                if t >= 4:
                    tt = t - 4
                    g.wait_ge(d_x1, tt + 1)
                    g.dma_start(out=out_d[tt * 128:(tt + 1) * 128, :],
                                in_=x1t[tt % 2][:]).then_inc(s_out1[tt % 2], 16)
                    g.wait_ge(d_ln2, tt + 1)
                    g.dma_start(out=ln2_sc[tt * 128:(tt + 1) * 128, :],
                                in_=ln2t[tt % 2][:]).then_inc(s_ln2w[tt % 2], 16)
            for tt in range(NT - 4, NT):
                g.wait_ge(d_x1, tt + 1)
                g.dma_start(out=out_d[tt * 128:(tt + 1) * 128, :],
                            in_=x1t[tt % 2][:]).then_inc(s_out1[tt % 2], 16)
                g.wait_ge(d_ln2, tt + 1)
                g.dma_start(out=ln2_sc[tt * 128:(tt + 1) * 128, :],
                            in_=ln2t[tt % 2][:]).then_inc(s_ln2w[tt % 2], 16)
            if upto < 6:
                for s_ in s_out1 + s_ln2w:
                    g.wait_ge(s_, 16 * (NT // 2))
                if debug:
                    g.wait_ge(d_kv2, 1)
                    g.dma_start(out=dbg_si, in_=si_all[:]).then_inc(s_dbg, 16)
                    g.dma_start(out=dbg_sia, in_=sia_all[:]).then_inc(s_dbg, 16)
                    g.dma_start(out=dbg_sc, in_=sc_tm[:]).then_inc(s_dbg, 16)
                    g.dma_start(out=dbg_kv, in_=kv2[:]).then_inc(s_dbg, 16)
                    g.wait_ge(s_dbg, 16 * 4)
                return
            g.wait_ge(s_out1[0], 16 * (NT // 2))
            g.wait_ge(s_out1[1], 16 * (NT // 2))
            g.wait_ge(s_ln2w[0], 16 * (NT // 2))
            g.wait_ge(s_ln2w[1], 16 * (NT // 2))
            g.dma_start(out=wd_a[:],
                        in_=wd_d[0:20 * 128, :].rearrange("(c p) d -> p c d", p=128)
                        ).then_inc(s_wd, 16)
            g.dma_start(out=wd_b[:],
                        in_=wd_d[20 * 128:, :].rearrange("(c p) d -> p c d", p=128)
                        ).then_inc(s_wd, 16)
            if debug:
                g.wait_ge(d_kv2, 1)
                g.dma_start(out=dbg_si, in_=si_all[:]).then_inc(s_dbg, 16)
                g.dma_start(out=dbg_sia, in_=sia_all[:]).then_inc(s_dbg, 16)
                g.dma_start(out=dbg_sc, in_=sc_tm[:]).then_inc(s_dbg, 16)
                g.dma_start(out=dbg_kv, in_=kv2[:]).then_inc(s_dbg, 16)
                g.dma_start(out=dbg_rows[0:1, 0:D], in_=ks_row[:]).then_inc(s_dbg, 16)
                g.dma_start(out=dbg_rows[0:1, D:2 * D], in_=qs_row[:]).then_inc(s_dbg, 16)
                g.dma_start(out=dbg_rows[0:1, 2 * D:3 * D], in_=sk_row[:]).then_inc(s_dbg, 16)
                g.dma_start(out=dbg_rows[0:1, 3 * D:4 * D], in_=sq_row[:]).then_inc(s_dbg, 16)
            for gi in range(NG):
                for t4 in range(4):
                    t = gi * 4 + t4
                    g.wait_ge(d_fo, 4 * gi + t4 + 1)
                    g.dma_start(out=out_d[t * 128:(t + 1) * 128, :],
                                in_=fout[t4 % 2][:], accum_op=ALU.add
                                ).then_inc(s_out2[t4 % 2], 16)
            g.wait_ge(s_out2[0], 16 * (NT // 2))
            g.wait_ge(s_out2[1], 16 * (NT // 2))
            if debug:
                g.wait_ge(s_dbg, 16 * 8)

        @block.sync
        def _(sy):
            for t in range(NT):
                sy.wait_ge(d_ln1, t + 1)
                if t >= 4:
                    sy.wait_ge(t_v, t - 3)
                for c in range(C):
                    sy.dma_start_transpose(lnh1[t % 4][:, c, :],
                                           ln1t[t % 3][:, c * 128:(c + 1) * 128]
                                           ).then_inc(s_lnT[t % 4], 16)
            if upto < 2:
                return
            sy.wait_ge(s_qw[0], 16 * (NT // 2))
            sy.wait_ge(s_qw[1], 16 * (NT // 2))
            sy.wait_ge(s_kw[0], 16 * (NT // 2))
            sy.wait_ge(s_kw[1], 16 * (NT // 2))
            for gi in range(NG):
                if gi >= 2:
                    sy.wait_ge(t_dB, (gi - 1) * 4)
                for c in range(C):
                    sy.dma_start_transpose(
                        qfm[gi % 2][:, c, :],
                        q_sc[gi * 512:(gi + 1) * 512, c * 128:(c + 1) * 128]
                    ).then_inc(s_fm2[gi % 2], 16)
                for c in range(C):
                    sy.dma_start_transpose(
                        kfm[gi % 2][:, c, :],
                        k_sc[gi * 512:(gi + 1) * 512, c * 128:(c + 1) * 128]
                    ).then_inc(s_fm2[gi % 2], 16)
            if upto < 3:
                return
            for gi in range(NG):
                if gi == 0:
                    sy.wait_ge(t_dB, 4 * (NG - 1))
                elif gi == 1:
                    sy.wait_ge(t_sum2, NT)
                else:
                    sy.wait_ge(t_D, (gi - 1) * 4)
                    sy.wait_ge(t_C, gi - 1)
                for c in range(C):
                    sy.dma_start_transpose(
                        qfm[gi % 2][:, c, :],
                        q_sc[gi * 512:(gi + 1) * 512, c * 128:(c + 1) * 128]
                    ).then_inc(s_fm3[gi % 2], 16)
                for c in range(C):
                    sy.dma_start_transpose(
                        kfm[gi % 2][:, c, :],
                        k_sc[gi * 512:(gi + 1) * 512, c * 128:(c + 1) * 128]
                    ).then_inc(s_fm3[gi % 2], 16)
            if upto < 6:
                return
            for gi in range(NG):
                if gi == 0:
                    sy.wait_ge(t_wo, NT)
                sy.wait_ge(s_ln2w[0], 16 * (2 * gi + 2))
                sy.wait_ge(s_ln2w[1], 16 * (2 * gi + 2))
                if gi >= 2:
                    sy.wait_ge(t_U, HC * (gi - 1))
                for c in range(C):
                    sy.dma_start_transpose(
                        lnh2[gi % 2][:, c, :],
                        ln2_sc[gi * 512:(gi + 1) * 512, c * 128:(c + 1) * 128]
                    ).then_inc(s_lh2[gi % 2], 16)

        @block.tensor
        def _(t_):
            t_.wait_ge(s_cst, 10)
            t_.wait_ge(s_w, 16 * 4)
            for t in range(NT):
                t_.wait_ge(s_lnT[t % 4], 16 * 8 * (t // 4 + 1))
                lnh = lnh1[t % 4]
                if t >= 1:
                    t_.wait_ge(d_vev, t)
                for c in range(C):
                    t_.matmul(P0[:], lnh[:, c, :], wq_sb[:, c, 0:512],
                              start=(c == 0), stop=False)
                    t_.matmul(P1[:], lnh[:, c, :], wq_sb[:, c, 512:1024],
                              start=(c == 0), stop=False)
                t_.matmul(P0[:], ones_r[0:1, 0:128], b_sb[0:1, 0:512],
                          start=False, stop=True)
                t_.matmul(P1[:], ones_r[0:1, 0:128], b_sb[0:1, 512:1024],
                          start=False, stop=True).then_inc(t_q, 1)
                if t >= 1:
                    t_.wait_ge(a_sig, 2 * t)
                for c in range(C):
                    t_.matmul(P2[:], lnh[:, c, :], wk_sb[:, c, 0:512],
                              start=(c == 0), stop=False)
                    t_.matmul(P3[:], lnh[:, c, :], wk_sb[:, c, 512:1024],
                              start=(c == 0), stop=False)
                t_.matmul(P2[:], ones_r[0:1, 0:128], b_sb[0:1, 1024:1536],
                          start=False, stop=True)
                t_.matmul(P3[:], ones_r[0:1, 0:128], b_sb[0:1, 1536:2048],
                          start=False, stop=True).then_inc(t_k, 1)
                t_.wait_ge(a_sig, 2 * t + 1)
                for c in range(C):
                    t_.matmul(P0[:], lnh[:, c, :], wv_sb[:, c, 0:512],
                              start=(c == 0), stop=False)
                    t_.matmul(P1[:], lnh[:, c, :], wv_sb[:, c, 512:1024],
                              start=(c == 0), stop=False)
                t_.matmul(P0[:], ones_r[0:1, 0:128], b_sb[0:1, 2048:2560],
                          start=False, stop=True)
                t_.matmul(P1[:], ones_r[0:1, 0:128], b_sb[0:1, 2560:3072],
                          start=False, stop=True).then_inc(t_v, 1)
                t_.wait_ge(a_sig, 2 * t + 2)
                t_.matmul(P4[0:1, :], ones_c[:], qt1[t % 2][:, 0:512],
                          start=(t == 0), stop=(t == NT - 1))
                t_.matmul(P5[0:1, :], ones_c[:], qt1[t % 2][:, 512:1024],
                          start=(t == 0), stop=(t == NT - 1))
                t_.matmul(P6[0:1, :], ones_c[:], kt1[t % 2][:, 0:512],
                          start=(t == 0), stop=(t == NT - 1))
                t_.matmul(P7[0:1, :], ones_c[:], kt1[t % 2][:, 512:1024],
                          start=(t == 0), stop=(t == NT - 1)).then_inc(t_sum, 1)
            if upto < 2:
                return
            pscol = bfv(P0, 32)
            t_.wait_ge(d_rows, 1)
            for c in range(C):
                t_.transpose(pscol[:, 2 * c:2 * c + 1],
                             ks_row[0:1, c * 128:(c + 1) * 128], identb[0:1, 0:1])
            for c in range(C):
                mmt = t_.transpose(pscol[:, 16 + 2 * c:16 + 2 * c + 1],
                                   qs_row[0:1, c * 128:(c + 1) * 128],
                                   identb[0:1, 0:1])
            mmt.then_inc(t_cols, 1)
            t_.wait_ge(d_blk, 1)
            psA16 = [P0[:, 0:16], P1[:, 0:16]]
            psB16 = [P2[:, 0:16], P3[:, 0:16]]
            for t in range(NT):
                gi = t // 4
                t_.wait_ge(s_fm2[gi % 2], 16 * 16 * (gi // 2 + 1))
                if t >= 2:
                    t_.wait_ge(d_rec, 2 * (t - 1))
                pa = psA16[t % 2]
                for c in range(C):
                    t_.matmul(pa[:, 2 * c:2 * c + 2],
                              qfm[gi % 2][:, c, (t % 4) * 128:(t % 4 + 1) * 128],
                              ksb[:, c, :], start=(c == 0), stop=False,
                              skip_group_check=True)
                t_.matmul(pa, ones_r[0:1, 0:128], eks_row[:],
                          start=False, stop=True, skip_group_check=True
                          ).then_inc(t_dA, 1)
                pb = psB16[t % 2]
                for c in range(C):
                    t_.matmul(pb[:, 2 * c:2 * c + 2],
                              kfm[gi % 2][:, c, (t % 4) * 128:(t % 4 + 1) * 128],
                              qsb[:, c, :], start=(c == 0), stop=False,
                              skip_group_check=True)
                t_.matmul(pb, ones_r[0:1, 0:128], eqs_row[:],
                          start=False, stop=True, skip_group_check=True
                          ).then_inc(t_dB, 1)
                t_.wait_ge(d_p2, t + 1)
                t_.matmul(P4[0:1, :], ones_c[:], qsi_t[0][:, 0:512],
                          start=(t == 0), stop=(t == NT - 1))
                t_.matmul(P5[0:1, :], ones_c[:], qsi_t[0][:, 512:1024],
                          start=(t == 0), stop=(t == NT - 1))
                t_.matmul(P6[0:1, :], ones_c[:], kso_t[0][:, 0:512],
                          start=(t == 0), stop=(t == NT - 1))
                t_.matmul(P7[0:1, :], ones_c[:], kso_t[0][:, 512:1024],
                          start=(t == 0), stop=(t == NT - 1)).then_inc(t_sum2, 1)
            t_.wait_ge(d_rows, 2)
            for c in range(C):
                t_.transpose(pscol[:, 2 * c:2 * c + 1],
                             sk_row[0:1, c * 128:(c + 1) * 128], identb[0:1, 0:1])
            for c in range(C):
                mmt = t_.transpose(pscol[:, 16 + 2 * c:16 + 2 * c + 1],
                                   sq_row[0:1, c * 128:(c + 1) * 128],
                                   identb[0:1, 0:1])
            mmt.then_inc(t_cols, 1)
            if upto < 3:
                return
            t_.wait_ge(d_blk, 2)
            t_.wait_ge(d_rec, 2 * NT)
            for gi in range(NG):
                t_.wait_ge(s_fm3[gi % 2], 16 * 16 * (gi // 2 + 1))
                if gi >= 2:
                    t_.wait_ge(d_C, gi - 1)
                pc = [P2[0:16, :], P3[0:16, :]][gi % 2]
                for c in range(C):
                    t_.matmul(pc, sqb16[:, c, :], kfm[gi % 2][:, c, :],
                              start=(c == 0), stop=False, skip_group_check=True)
                t_.matmul(pc, esq_row[:], ones_r[:], start=False, stop=True,
                          skip_group_check=True).then_inc(t_C, 1)
                for t4 in range(4):
                    t = gi * 4 + t4
                    if t >= 2:
                        t_.wait_ge(a_sa, t - 1)
                    pd = psA16[t % 2]
                    for c in range(C):
                        t_.matmul(pd[:, 2 * c:2 * c + 2],
                                  qfm[gi % 2][:, c, t4 * 128:(t4 + 1) * 128],
                                  skb[:, c, :], start=(c == 0), stop=False,
                                  skip_group_check=True)
                    t_.matmul(pd, ones_r[0:1, 0:128], esk_row[:],
                              start=False, stop=True, skip_group_check=True
                              ).then_inc(t_D, 1)
            t_.wait_ge(d_scf, 1)
            psT4 = bfv(P4, 512)
            mmt = None
            for i in range(NT):
                mmt = t_.transpose(psT4[:, 16 * i:16 * (i + 1)],
                                   exp_fm[0:16, i * 128:(i + 1) * 128],
                                   identb[0:16, 0:16])
            mmt.then_inc(t_scT, NT)
            if upto < 4:
                return
            kvps = [P2, P3]
            for t in range(NT):
                t_.wait_ge(s_k4[t % 3], 16 * (t // 3 + 1))
                t_.wait_ge(d_vsc, t + 1)
                mm = None
                for c in range(C):
                    mm = t_.matmul(
                        kvps[c // 4][:, (c % 4) * 128:(c % 4 + 1) * 128],
                        kst[t % 3][:, c * 128:(c + 1) * 128],
                        vsc_t[t % 2][:, c * 128:(c + 1) * 128],
                        start=(t == 0 and c % 4 == 0),
                        stop=(t == NT - 1 and c % 4 == 3),
                        skip_group_check=True)
                mm.then_inc(t_kv4, 1)
            if upto < 5:
                return
            t_.wait_ge(d_kv2, 1)
            t_.wait_ge(d_sct, NT)
            t_.wait_ge(s_wo, 16)
            psO = [P0, P1]
            pqx = [bfv(P2, 1024), bfv(P3, 1024)]
            for gi in range(NG):
                for t4 in range(4):
                    t = gi * 4 + t4
                    t_.wait_ge(d_qx, t + 1)
                    if t >= 2:
                        t_.wait_ge(a_qx, t - 1)
                    mmq = None
                    for c in range(C):
                        mmq = t_.transpose(pqx[t % 2][:, c * 128:(c + 1) * 128],
                                           qx_t[t % 2][:, c * 128:(c + 1) * 128],
                                           identb[:, :])
                    mmq.then_inc(t_qxT, 1)
                t_.wait_ge(a_qx, 4 * (gi + 1))
                for c in range(C):
                    idx = gi * C + c
                    if idx >= 2:
                        prev = idx - 2
                        if prev % 2 == 0:
                            t_.wait_ge(e5a, prev // 2 + 1)
                        else:
                            t_.wait_ge(e5d, prev // 2 + 1)
                    t_.matmul(psO[c % 2][:], kv2[:, c, :], qxfm[gi % 3][:, c, :],
                              start=True, stop=True).then_inc(t_ko, 1)
                t_.wait_ge(e5a, 4 * (gi + 1))
                t_.wait_ge(e5d, 4 * (gi + 1))
                for t4 in range(4):
                    t = gi * 4 + t4
                    if t >= 2:
                        t_.wait_ge(d_x1, t - 1)
                    pw0, pw1 = [(P4, P5), (P6, P7)][t % 2]
                    for c in range(C):
                        t_.matmul(pw0[:], qxfm[gi % 3][:, c, t4 * 128:(t4 + 1) * 128],
                                  wo_sb[:, c, 0:512], start=(c == 0), stop=False)
                        t_.matmul(pw1[:], qxfm[gi % 3][:, c, t4 * 128:(t4 + 1) * 128],
                                  wo_sb[:, c, 512:1024], start=(c == 0), stop=False)
                    t_.matmul(pw0[:], ones_r[0:1, 0:128], b_sb[0:1, 3072:3584],
                              start=False, stop=True)
                    t_.matmul(pw1[:], ones_r[0:1, 0:128], b_sb[0:1, 3584:4096],
                              start=False, stop=True).then_inc(t_wo, 1)
            if upto < 6:
                return
            t_.wait_ge(d_x1, NT)
            t_.wait_ge(s_wf, 32)
            psG = [P0, P1]
            psU = [P2, P3]
            for gi in range(NG):
                t_.wait_ge(s_lh2[gi % 2], 16 * 8 * (gi // 2 + 1))
                for hc in range(HC):
                    idx = gi * HC + hc
                    if idx >= 2:
                        t_.wait_ge(a_sg, idx - 1)
                    mmg = None
                    for j in range(C):
                        mmg = t_.matmul(psG[hc % 2][:],
                                        wg_sb[:, j, hc * 128:(hc + 1) * 128],
                                        lnh2[gi % 2][:, j, :],
                                        start=(j == 0), stop=(j == C - 1))
                    mmg.then_inc(t_G, 1)
                    if idx >= 2:
                        t_.wait_ge(d_h, idx - 1)
                    mmu = None
                    for j in range(C):
                        mmu = t_.matmul(psU[hc % 2][:],
                                        wu_sb[:, j, hc * 128:(hc + 1) * 128],
                                        lnh2[gi % 2][:, j, :],
                                        start=(j == 0), stop=(j == C - 1))
                    mmu.then_inc(t_U, 1)
                t_.wait_ge(d_h, HC * (gi + 1))
                if gi == 0:
                    t_.wait_ge(s_wd, 32)
                for t4 in range(4):
                    gt = gi * 4 + t4
                    if gt >= 2:
                        t_.wait_ge(d_fo, gt - 1)
                    pd0, pd1 = [(P4, P5), (P6, P7)][t4 % 2]
                    def wd_at(hc):
                        return wd_a[:, hc, :] if hc < 20 else wd_b[:, hc - 20, :]
                    for hc in range(HC):
                        t_.matmul(pd0[:], h_fm[:, hc, t4 * 128:(t4 + 1) * 128],
                                  wd_at(hc)[:, 0:512], start=(hc == 0),
                                  stop=(hc == HC - 1))
                    mmd = None
                    for hc in range(HC):
                        mmd = t_.matmul(pd1[:], h_fm[:, hc, t4 * 128:(t4 + 1) * 128],
                                        wd_at(hc)[:, 512:1024], start=(hc == 0),
                                        stop=(hc == HC - 1))
                    mmd.then_inc(t_dn, 1)

        @block.scalar
        def _(s):
            s.wait_ge(s_cst, 10)
            for t in range(NT):
                s.wait_ge(s_x1[t % 3], 16 * (t // 3 + 1))
                if t >= 4:
                    s.wait_ge(d_vpe, t - 3)
                s.activation(sq_scr[:], xt1[t % 3][:], AF.Square,
                             accum_out=ssq[:, t % 4:t % 4 + 1]).then_inc(a_sq, 1)
                s.activation(sq_scr[:], xt1[t % 3][:], AF.Copy,
                             accum_out=nsum[:, t % 4:t % 4 + 1]).then_inc(a_sq, 1)
                s.wait_ge(d_vpe, t + 1)
                if t >= 4:
                    s.wait_ge(d_ln1, t - 3)
                s.activation(sigc[:, t % 4:t % 4 + 1], vpe[:, t % 4:t % 4 + 1],
                             AF.Sqrt).then_inc(a_sqr, 1)
                s.wait_ge(t_q, t + 1)
                if t >= 2:
                    s.wait_ge(t_sum, t - 1)
                    s.wait_ge(s_qw[t % 2], 16 * (t // 2))
                s.activation(qt1[t % 2][:, 0:512], P0[:], AF.Sigmoid)
                s.activation(qt1[t % 2][:, 512:1024], P1[:], AF.Sigmoid
                             ).then_inc(a_sig, 1)
                s.wait_ge(t_k, t + 1)
                if t >= 2:
                    s.wait_ge(s_kw[t % 2], 16 * (t // 2))
                s.activation(kt1[t % 2][:, 0:512], P2[:], AF.Sigmoid)
                s.activation(kt1[t % 2][:, 512:1024], P3[:], AF.Sigmoid
                             ).then_inc(a_sig, 1)
            if upto < 3:
                return
            psA16 = [P0[:, 0:16], P1[:, 0:16]]
            for t in range(NT):
                s.wait_ge(t_D, t + 1)
                if t >= 2:
                    s.wait_ge(d_sia, t - 1)
                s.activation(sa_t[:, t % 2, :], psA16[t % 2], AF.Sigmoid
                             ).then_inc(a_sa, 1)
            s.wait_ge(d_mx, 1)
            s.activation(exp_fm[:], csrc_fm[:], AF.Exp, bias=nmx16[:],
                         accum_out=se16[:]).then_inc(a_exp, 1)
            if upto < 5:
                return
            s.wait_ge(d_kv2, 1)
            for t in range(3):
                s.dma_start(out=xt5[t % 3][:], in_=x_d[t * 128:(t + 1) * 128, :]
                            ).then_inc(s_x5[t % 3], 16)
            pqx = [bfv(P2, 1024), bfv(P3, 1024)]
            for gi in range(NG):
                for t4 in range(4):
                    t = gi * 4 + t4
                    s.wait_ge(t_qxT, t + 1)
                    s.activation(qxfm[gi % 3][:, 0:4, t4 * 128:(t4 + 1) * 128],
                                 pqx[t % 2][:, 0:512].rearrange(
                                     "p (c x) -> p c x", c=4), AF.Copy)
                    s.activation(qxfm[gi % 3][:, 4:8, t4 * 128:(t4 + 1) * 128],
                                 pqx[t % 2][:, 512:1024].rearrange(
                                     "p (c x) -> p c x", c=4), AF.Copy
                                 ).then_inc(a_qx, 1)
                for c in range(0, C, 2):
                    s.wait_ge(t_ko, gi * C + c + 1)
                    s.activation(qxfm[gi % 3][:, c, :], [P0, P1][c % 2][:],
                                 AF.Copy).then_inc(e5a, 1)
                for t4 in range(4):
                    t = gi * 4 + t4
                    s.wait_ge(d_x1, t + 1)
                    if t >= 4:
                        s.wait_ge(d_vp2, t - 3)
                    s.activation(sq_scr2[:], x1t[t % 2][:], AF.Square,
                                 accum_out=ssq[:, t % 4:t % 4 + 1]).then_inc(a_sq2, 1)
                    s.activation(sq_scr2[:], x1t[t % 2][:], AF.Copy,
                                 accum_out=nsum[:, t % 4:t % 4 + 1]).then_inc(a_sq2, 1)
                    if t + 3 < NT:
                        s.dma_start(out=xt5[(t + 3) % 3][:],
                                    in_=x_d[(t + 3) * 128:(t + 4) * 128, :]
                                    ).then_inc(s_x5[t % 3], 16)
                    s.wait_ge(d_vp2, t + 1)
                    if t >= 4:
                        s.wait_ge(d_ln2, t - 3)
                    s.activation(sigc[:, t % 4:t % 4 + 1], vpe[:, t % 4:t % 4 + 1],
                                 AF.Sqrt).then_inc(a_sr2, 1)
            if upto < 6:
                return
            for gi in range(NG):
                for hc in range(HC):
                    idx = gi * HC + hc
                    s.wait_ge(t_G, idx + 1)
                    if idx >= 2:
                        s.wait_ge(d_h, idx - 1)
                    s.activation(sg_t[hc % 2][:], [P0, P1][hc % 2][:],
                                 AF.Silu, scale=c127[:]).then_inc(a_sg, 1)

        @block.vector
        def _(v):
            v.wait_ge(s_cst, 10)
            for t in range(NT + 1):
                if t >= 1:
                    tt = t - 1
                    v.wait_ge(a_sqr, tt + 1)
                    dsc[0] += 1
                    v.reciprocal(rsig[:, tt % 4:tt % 4 + 1],
                                 sigc[:, tt % 4:tt % 4 + 1]).then_inc(d_self, 1)
                    v.wait_ge(d_self, dsc[0])
                    if tt >= 3:
                        v.wait_ge(s_lnT[(tt - 3) % 4], 16 * 8 * ((tt - 3) // 4 + 1))
                    v.tensor_scalar(ln1t[tt % 3][:], xt1[tt % 3][:],
                                    negmu[:, tt % 4:tt % 4 + 1],
                                    rsig[:, tt % 4:tt % 4 + 1],
                                    ALU.add, ALU.mult).then_inc(d_ln1, 1)
                if t < NT:
                    if t >= 1:
                        tt = t - 1
                        v.wait_ge(t_v, tt + 1)
                        if tt >= 2:
                            v.wait_ge(s_vw[tt % 2], 16 * (tt // 2))
                        v.tensor_copy(vt1[tt % 2][:, 0:512], P0[:])
                        v.tensor_copy(vt1[tt % 2][:, 512:1024], P1[:]
                                      ).then_inc(d_vev, 1)
                    v.wait_ge(a_sq, 2 * t + 2)
                    v.tensor_scalar(negmu[:, t % 4:t % 4 + 1],
                                    nsum[:, t % 4:t % 4 + 1], -1.0 / D,
                                    None, ALU.mult)
                    dsc[0] += 1
                    v.tensor_scalar(e2c[:], ssq[:, t % 4:t % 4 + 1], 1.0 / D,
                                    None, ALU.mult).then_inc(d_self, 1)
                    v.wait_ge(d_self, dsc[0])
                    dsc[0] += 1
                    v.scalar_tensor_tensor(nvc[:], negmu[:, t % 4:t % 4 + 1],
                                           negmu[:, t % 4:t % 4 + 1], e2c[:],
                                           ALU.mult, ALU.subtract
                                           ).then_inc(d_self, 1)
                    v.wait_ge(d_self, dsc[0])
                    v.tensor_scalar(vpe[:, t % 4:t % 4 + 1], nvc[:], -1.0, LNEPS,
                                    ALU.mult, ALU.add).then_inc(d_vpe, 1)
            tt = NT - 1
            v.wait_ge(t_v, tt + 1)
            v.tensor_copy(vt1[tt % 2][:, 0:512], P0[:])
            v.tensor_copy(vt1[tt % 2][:, 512:1024], P1[:]).then_inc(d_vev, 1)
            if upto < 2:
                return
            v.wait_ge(t_sum, NT)
            v.tensor_copy(qs_row[0:1, 0:512], P4[0:1, :])
            v.tensor_copy(qs_row[0:1, 512:1024], P5[0:1, :])
            v.tensor_copy(ks_row[0:1, 0:512], P6[0:1, :])
            v.tensor_copy(ks_row[0:1, 512:1024], P7[0:1, :]).then_inc(d_rows, 1)
            pscol = bfv(P0, 32)
            v.wait_ge(t_cols, 1)
            for c in range(C):
                v.tensor_scalar(ksb[0:64, c, 0:1], pscol[0:64, 2 * c:2 * c + 1],
                                EPS, None, ALU.add)
                v.tensor_scalar(ksb[64:128, c, 1:2], pscol[64:128, 2 * c:2 * c + 1],
                                EPS, None, ALU.add)
                v.tensor_scalar(qsb[0:64, c, 0:1],
                                pscol[0:64, 16 + 2 * c:16 + 2 * c + 1], EPS,
                                None, ALU.add)
                v.tensor_scalar(qsb[64:128, c, 1:2],
                                pscol[64:128, 16 + 2 * c:16 + 2 * c + 1],
                                EPS, None, ALU.add)
            v.reduce_sum(ep_tmp[0:1, :].unsqueeze(2),
                         ks_row[0:1, :].rearrange("p (h d) -> p h d", h=H),
                         axis=AX.X, opt_input=False, opt_output=False
                         ).then_inc(d_self, 1)
            dsc[0] += 1
            v.wait_ge(d_self, dsc[0])
            dsc[0] += 1
            v.tensor_scalar(ep_tmp[:], ep_tmp[:], 64.0 * EPS, EPS, ALU.add,
                            ALU.mult).then_inc(d_self, 1)
            v.wait_ge(d_self, dsc[0])
            v.tensor_copy(eks_row[:], ep_tmp[:])
            v.reduce_sum(ep_tmp[0:1, :].unsqueeze(2),
                         qs_row[0:1, :].rearrange("p (h d) -> p h d", h=H),
                         axis=AX.X, opt_input=False, opt_output=False
                         ).then_inc(d_self, 1)
            dsc[0] += 1
            v.wait_ge(d_self, dsc[0])
            dsc[0] += 1
            v.tensor_scalar(ep_tmp[:], ep_tmp[:], 64.0 * EPS, EPS, ALU.add,
                            ALU.mult).then_inc(d_self, 1)
            v.wait_ge(d_self, dsc[0])
            v.tensor_copy(eqs_row[:], ep_tmp[:]).then_inc(d_blk, 1)
            psA16 = [P0[:, 0:16], P1[:, 0:16]]
            psB16 = [P2[:, 0:16], P3[:, 0:16]]
            for t in range(NT):
                v.wait_ge(t_dA, t + 1)
                v.reciprocal(si_all[:, t, :], psA16[t % 2])
                v.wait_ge(t_dB, t + 1)
                dsc[0] += 1
                v.reciprocal(so_t[:, t % 2, :], psB16[t % 2]
                             ).then_inc(d_self, 1)
                v.wait_ge(d_self, dsc[0])
                v.engine_nop().then_inc(d_rec, 2)
                v.wait_ge(s_q2[t % 2], 16 * (t // 2 + 1))
                v.wait_ge(s_k2[t % 2], 16 * (t // 2 + 1))
                if t >= 1:
                    v.wait_ge(t_sum2, t)
                v.tensor_tensor(
                    qsi_t[0][:, :].rearrange("p (h d) -> p h d", h=H),
                    qst[t % 2][:, :].rearrange("p (h d) -> p h d", h=H),
                    si_all[:, t, :].unsqueeze(2).broadcast_to([128, H, HD]),
                    ALU.mult)
                v.tensor_tensor(
                    kso_t[0][:, :].rearrange("p (h d) -> p h d", h=H),
                    kst2[t % 2][:, :].rearrange("p (h d) -> p h d", h=H),
                    so_t[:, t % 2, :].unsqueeze(2).broadcast_to([128, H, HD]),
                    ALU.mult).then_inc(d_p2, 1)
            v.wait_ge(t_sum2, NT)
            v.tensor_copy(sq_row[0:1, 0:512], P4[0:1, :])
            v.tensor_copy(sq_row[0:1, 512:1024], P5[0:1, :])
            v.tensor_copy(sk_row[0:1, 0:512], P6[0:1, :])
            v.tensor_copy(sk_row[0:1, 512:1024], P7[0:1, :]).then_inc(d_rows, 1)
            v.wait_ge(t_cols, 2)
            for c in range(C):
                v.tensor_scalar(skb[0:64, c, 0:1], pscol[0:64, 2 * c:2 * c + 1],
                                EPS, None, ALU.add)
                v.tensor_scalar(skb[64:128, c, 1:2], pscol[64:128, 2 * c:2 * c + 1],
                                EPS, None, ALU.add)
                v.tensor_scalar(sqb16[0:64, c, 2 * c:2 * c + 1],
                                pscol[0:64, 16 + 2 * c:16 + 2 * c + 1], EPS,
                                None, ALU.add)
                v.tensor_scalar(sqb16[64:128, c, 2 * c + 1:2 * c + 2],
                                pscol[64:128, 16 + 2 * c:16 + 2 * c + 1],
                                EPS, None, ALU.add)
            v.reduce_sum(ep_tmp[0:1, :].unsqueeze(2),
                         sk_row[0:1, :].rearrange("p (h d) -> p h d", h=H),
                         axis=AX.X, opt_input=False, opt_output=False
                         ).then_inc(d_self, 1)
            dsc[0] += 1
            v.wait_ge(d_self, dsc[0])
            dsc[0] += 1
            v.tensor_scalar(ep_tmp[:], ep_tmp[:], 64.0 * EPS, EPS, ALU.add,
                            ALU.mult).then_inc(d_self, 1)
            v.wait_ge(d_self, dsc[0])
            v.tensor_copy(esk_row[:], ep_tmp[:])
            v.reduce_sum(ep_tmp[0:1, :].unsqueeze(2),
                         sq_row[0:1, :].rearrange("p (h d) -> p h d", h=H),
                         axis=AX.X, opt_input=False, opt_output=False
                         ).then_inc(d_self, 1)
            dsc[0] += 1
            v.wait_ge(d_self, dsc[0])
            dsc[0] += 1
            v.tensor_scalar(ep_tmp[:], ep_tmp[:], 64.0 * EPS, EPS, ALU.add,
                            ALU.mult).then_inc(d_self, 1)
            v.wait_ge(d_self, dsc[0])
            v.tensor_copy(esq_row[:], ep_tmp[:]).then_inc(d_blk, 1)
            if upto < 3:
                return
            for gi in range(NG):
                v.wait_ge(t_C, gi + 1)
                pc = [P2[0:16, :], P3[0:16, :]][gi % 2]
                v.tensor_scalar(csrc_fm[:, gi * 512:(gi + 1) * 512], pc,
                                1.0, -1.0, ALU.min, ALU.max).then_inc(d_C, 1)
                for t4 in range(4):
                    t = gi * 4 + t4
                    v.wait_ge(a_sa, t + 1)
                    v.tensor_tensor(sia_all[:, t, :], si_all[:, t, :],
                                    sa_t[:, t % 2, :], ALU.mult).then_inc(d_sia, 1)
            v.wait_ge(d_C, NG)
            v.reduce_max(mx16[:], csrc_fm[:], axis=AX.X, opt_input=False,
                         opt_output=False).then_inc(d_self, 1)
            dsc[0] += 1
            v.wait_ge(d_self, dsc[0])
            v.tensor_scalar(nmx16[:], mx16[:], -1.0, None, ALU.mult).then_inc(d_mx, 1)
            v.wait_ge(a_exp, 1)
            dsc[0] += 1
            v.reciprocal(rse16[:], se16[:]).then_inc(d_self, 1)
            v.wait_ge(d_self, dsc[0])
            v.tensor_scalar(exp_fm[:], exp_fm[:], rse16[:], SFL,
                            ALU.mult, ALU.mult).then_inc(d_scf, 1)
            psT4 = bfv(P4, 512)
            v.wait_ge(t_scT, NT)
            v.tensor_copy(sc_tm[:, :, :],
                          psT4[:, :].rearrange("p (i h) -> p i h", i=NT)
                          ).then_inc(d_sct, NT)
            if upto < 4:
                return
            for t in range(NT):
                v.wait_ge(s_v4[t % 3], 16 * (t // 3 + 1))
                if t >= 2:
                    v.wait_ge(t_kv4, t - 1)
                v.tensor_tensor(
                    vsc_t[t % 2][:, :].rearrange("p (h d) -> p h d", h=H),
                    vst[t % 3][:, :].rearrange("p (h d) -> p h d", h=H),
                    sc_tm[:, t, :].unsqueeze(2).broadcast_to([128, H, HD]),
                    ALU.mult).then_inc(d_vsc, 1)
            v.wait_ge(t_kv4, NT)
            for c in range(C):
                ps = [P2, P3][c // 4]
                off = (c % 4) * 128
                v.tensor_copy(kv2[0:64, c, 0:64], ps[0:64, off:off + 64])
                if c == C - 1:
                    v.tensor_copy(kv2[64:128, c, 64:128],
                                  ps[64:128, off + 64:off + 128]).then_inc(d_kv2, 1)
                else:
                    v.tensor_copy(kv2[64:128, c, 64:128],
                                  ps[64:128, off + 64:off + 128])
            if upto < 5:
                return
            for gi in range(NG):
                for t4 in range(4):
                    t = gi * 4 + t4
                    v.wait_ge(s_q5[t % 4], 16 * (t // 4 + 1))
                    if t >= 2:
                        v.wait_ge(t_qxT, t - 1)
                    v.tensor_tensor(
                        qx_t[t % 2][:, :].rearrange("p (h d) -> p h d", h=H),
                        qst5[t % 4][:, :].rearrange("p (h d) -> p h d", h=H),
                        sia_all[:, t, :].unsqueeze(2).broadcast_to([128, H, HD]),
                        ALU.mult).then_inc(d_qx, 1)
                for c in range(1, C, 2):
                    v.wait_ge(t_ko, gi * C + c + 1)
                    v.tensor_copy(qxfm[gi % 3][:, c, :], [P0, P1][c % 2][:]
                                  ).then_inc(e5d, 1)
                for t4 in range(4):
                    t = gi * 4 + t4
                    v.wait_ge(t_wo, t + 1)
                    v.wait_ge(s_x5[t % 3], 16 * (t // 3 + 1))
                    if t >= 2:
                        v.wait_ge(s_out1[t % 2], 16 * (t // 2))
                        v.wait_ge(a_sq2, 2 * (t - 1))
                    pw0, pw1 = [(P4, P5), (P6, P7)][t % 2]
                    v.scalar_tensor_tensor(x1t[t % 2][:, 0:512], pw0[:], 1.0,
                                           xt5[t % 3][:, 0:512], ALU.mult, ALU.add)
                    v.scalar_tensor_tensor(x1t[t % 2][:, 512:1024], pw1[:], 1.0,
                                           xt5[t % 3][:, 512:1024], ALU.mult,
                                           ALU.add).then_inc(d_x1, 1)
                    v.wait_ge(a_sq2, 2 * t + 2)
                    v.tensor_scalar(negmu[:, t % 4:t % 4 + 1],
                                    nsum[:, t % 4:t % 4 + 1], -1.0 / D,
                                    None, ALU.mult)
                    dsc[0] += 1
                    v.tensor_scalar(e2c[:], ssq[:, t % 4:t % 4 + 1], 1.0 / D,
                                    None, ALU.mult).then_inc(d_self, 1)
                    v.wait_ge(d_self, dsc[0])
                    dsc[0] += 1
                    v.scalar_tensor_tensor(nvc[:], negmu[:, t % 4:t % 4 + 1],
                                           negmu[:, t % 4:t % 4 + 1], e2c[:],
                                           ALU.mult, ALU.subtract
                                           ).then_inc(d_self, 1)
                    v.wait_ge(d_self, dsc[0])
                    v.tensor_scalar(vpe[:, t % 4:t % 4 + 1], nvc[:], -1.0, LNEPS,
                                    ALU.mult, ALU.add).then_inc(d_vp2, 1)
                    if t >= 1:
                        tt = t - 1
                        v.wait_ge(a_sr2, tt + 1)
                        dsc[0] += 1
                        v.reciprocal(rsig[:, tt % 4:tt % 4 + 1],
                                     sigc[:, tt % 4:tt % 4 + 1]).then_inc(d_self, 1)
                        v.wait_ge(d_self, dsc[0])
                        if tt >= 2:
                            v.wait_ge(s_ln2w[tt % 2], 16 * (tt // 2))
                        v.tensor_scalar(ln2t[tt % 2][:], x1t[tt % 2][:],
                                        negmu[:, tt % 4:tt % 4 + 1],
                                        rsig[:, tt % 4:tt % 4 + 1],
                                        ALU.add, ALU.mult).then_inc(d_ln2, 1)
                if gi == NG - 1:
                    tt = NT - 1
                    v.wait_ge(a_sr2, tt + 1)
                    dsc[0] += 1
                    v.reciprocal(rsig[:, tt % 4:tt % 4 + 1],
                                 sigc[:, tt % 4:tt % 4 + 1]).then_inc(d_self, 1)
                    v.wait_ge(d_self, dsc[0])
                    v.wait_ge(s_ln2w[tt % 2], 16 * (tt // 2))
                    v.tensor_scalar(ln2t[tt % 2][:], x1t[tt % 2][:],
                                    negmu[:, tt % 4:tt % 4 + 1],
                                    rsig[:, tt % 4:tt % 4 + 1],
                                    ALU.add, ALU.mult).then_inc(d_ln2, 1)
            if upto < 6:
                return
            for gi in range(NG):
                for hc in range(HC):
                    idx = gi * HC + hc
                    v.wait_ge(a_sg, idx + 1)
                    v.wait_ge(t_U, idx + 1)
                    if hc == 0 and gi >= 1:
                        v.wait_ge(t_dn, 4 * gi)
                    v.tensor_tensor(h_fm[:, hc, :], sg_t[hc % 2][:],
                                    [P2, P3][hc % 2][:], ALU.mult).then_inc(d_h, 1)
                for t4 in range(4):
                    gt = gi * 4 + t4
                    v.wait_ge(t_dn, gt + 1)
                    if gt >= 2:
                        v.wait_ge(s_out2[gt % 2], 16 * (gt // 2))
                    pd0, pd1 = [(P4, P5), (P6, P7)][t4 % 2]
                    v.tensor_scalar(fout[t4 % 2][:, 0:512], pd0[:], DSC, None,
                                    ALU.mult)
                    v.tensor_scalar(fout[t4 % 2][:, 512:1024], pd1[:], DSC, None,
                                    ALU.mult).then_inc(d_fo, 1)

    return nc


def kernel(L0_attn_x, L0_attn_Wq, L0_attn_bq, L0_attn_Wk, L0_attn_bk,
           L0_attn_Wv, L0_attn_bv, L0_attn_Wo, L0_attn_bo,
           L0_ffn_w_gate, L0_ffn_w_up, L0_ffn_w_down):
    x = np.ascontiguousarray(np.asarray(L0_attn_x, dtype=np.float32))
    wqT = np.ascontiguousarray(np.asarray(L0_attn_Wq, np.float32).T)
    wkT = np.ascontiguousarray(np.asarray(L0_attn_Wk, np.float32).T)
    wvT = np.ascontiguousarray(np.asarray(L0_attn_Wv, np.float32).T)
    woT = np.ascontiguousarray(np.asarray(L0_attn_Wo, np.float32).T)
    bcat = np.ascontiguousarray(np.concatenate([
        np.asarray(L0_attn_bq, np.float32), np.asarray(L0_attn_bk, np.float32),
        np.asarray(L0_attn_bv, np.float32), np.asarray(L0_attn_bo, np.float32),
    ])[None, :])
    wgT = np.ascontiguousarray(np.asarray(L0_ffn_w_gate, np.int32).T)
    wuT = np.ascontiguousarray(np.asarray(L0_ffn_w_up, np.int32).T)
    wdT = np.ascontiguousarray(np.asarray(L0_ffn_w_down, np.int32).T)

    nc = build_program()
    in_maps = []
    for c in range(8):
        in_maps.append(dict(x=np.ascontiguousarray(x[c]), wqT=wqT, wkT=wkT,
                            wvT=wvT, woT=woT, bcat=bcat, wgT=wgT, wuT=wuT,
                            wdT=wdT))
    res = run_bass_kernel_spmd(nc, in_maps, core_ids=list(range(8)))
    out = np.stack([np.asarray(r["out"]) for r in res.results], axis=0)
    return out.astype(np.float32)


# version 05
# speedup vs baseline: 1.0173x; 1.0173x over previous
import contextlib
import numpy as np
import ml_dtypes
import concourse.bass as bass
import concourse.mybir as mybir
from concourse.bass_utils import run_bass_kernel_spmd

F32 = mybir.dt.float32
BF16 = mybir.dt.bfloat16
I32 = mybir.dt.int32
ALU = mybir.AluOpType
AF = mybir.ActivationFunctionType
AX = mybir.AxisListType
BF = ml_dtypes.bfloat16

T = 4096
D = 1024
NT = 32
NG = 8
C = 8
H = 16
HD = 64
HID = 3072
HC = 24
EPS = 1e-6
LNEPS = 1e-5
QSC = 1.0 / 127.0
DSC = 1.0 / (127.0 * 127.0)
SFL = 4096.0

KiB = 1024
ARENA = 207 * KiB


def build_program(debug=False, upto=6):
    nc = bass.Bass("TRN2", target_bir_lowering=False,
                   detect_race_conditions=False)

    x_d = nc.dram_tensor("x", [T, D], F32, kind="ExternalInput").ap()
    wq_d = nc.dram_tensor("wqT", [D, D], F32, kind="ExternalInput").ap()
    wk_d = nc.dram_tensor("wkT", [D, D], F32, kind="ExternalInput").ap()
    wv_d = nc.dram_tensor("wvT", [D, D], F32, kind="ExternalInput").ap()
    wo_d = nc.dram_tensor("woT", [D, D], F32, kind="ExternalInput").ap()
    b_d = nc.dram_tensor("bcat", [1, 4 * D], F32, kind="ExternalInput").ap()
    wg_d = nc.dram_tensor("wgT", [D, HID], I32, kind="ExternalInput").ap()
    wu_d = nc.dram_tensor("wuT", [D, HID], I32, kind="ExternalInput").ap()
    wd_d = nc.dram_tensor("wdT", [HID, D], I32, kind="ExternalInput").ap()
    out_d = nc.dram_tensor("out", [T, D], F32, kind="ExternalOutput").ap()

    skind = "ExternalOutput" if debug else "Internal"
    q_sc = nc.dram_tensor("q_sc", [T, D], BF16, kind=skind).ap()
    k_sc = nc.dram_tensor("k_sc", [T, D], BF16, kind=skind).ap()
    v_sc = nc.dram_tensor("v_sc", [T, D], BF16, kind=skind).ap()
    ln2_sc = nc.dram_tensor("ln2_sc", [T, D], BF16, kind=skind).ap()
    if debug:
        dbg_si = nc.dram_tensor("dbg_si", [128, NT, H], F32, kind="ExternalOutput").ap()
        dbg_sia = nc.dram_tensor("dbg_sia", [128, NT, H], BF16, kind="ExternalOutput").ap()
        dbg_sc = nc.dram_tensor("dbg_sc", [128, NT, H], BF16, kind="ExternalOutput").ap()
        dbg_kv = nc.dram_tensor("dbg_kv", [128, C, 128], BF16, kind="ExternalOutput").ap()
        dbg_rows = nc.dram_tensor("dbg_rows", [1, 4 * D], BF16, kind="ExternalOutput").ap()
        dbg_ln1 = nc.dram_tensor("dbg_ln1", [128, D], BF16, kind="ExternalOutput").ap()
        dbg_st = nc.dram_tensor("dbg_st", [128, 8], F32, kind="ExternalOutput").ap()
        dbg_lnh = nc.dram_tensor("dbg_lnh", [128, C, 128], BF16, kind="ExternalOutput").ap()

    ctx = contextlib.ExitStack()
    with ctx:
        arena_h = ctx.enter_context(
            nc.sbuf_tensor("arena", [128, ARENA], mybir.dt.uint8))
        arena_base = nc.lookup_mloc(arena_h).addr

        def place(name, shape, dtype, off):
            assert off + int(np.prod(shape[1:])) * mybir.dt.size(dtype) <= ARENA, name
            nc.dbg_offsets = getattr(nc, "dbg_offsets", {})
            nc.dbg_offsets[name] = (off, tuple(shape), dtype)
            h = nc.virtual_sbuf_tensor(name, shape, dtype)
            nc._tensor(h.name, h.shape, h.dtype, type="SB")
            mloc = nc.lookup_mloc(h)
            mloc.allocated = True
            mloc.addr = arena_base + off
            return h

        o = [0]
        def perm(name, shape, dtype):
            h = place(name, shape, dtype, o[0])
            o[0] += (int(np.prod(shape[1:])) * mybir.dt.size(dtype) + 31) // 32 * 32
            return h

        ones_r = perm("ones_r", [1, 512], BF16)
        ones_c = perm("ones_c", [128, 1], BF16)
        identb = perm("identb", [128, 128], BF16)
        b_sb = perm("b_sb", [1, 4 * D], BF16)
        ks_row = perm("ks_row", [1, D], BF16)
        qs_row = perm("qs_row", [1, D], BF16)
        sk_row = perm("sk_row", [1, D], BF16)
        sq_row = perm("sq_row", [1, D], BF16)
        ksb = perm("ksb", [128, C, 2], BF16)
        qsb = perm("qsb", [128, C, 2], BF16)
        skb = perm("skb", [128, C, 2], BF16)
        sqb16 = perm("sqb16", [128, C, H], BF16)
        eks_row = perm("eks_row", [1, H], BF16)
        eqs_row = perm("eqs_row", [1, H], BF16)
        esk_row = perm("esk_row", [1, H], BF16)
        esq_row = perm("esq_row", [1, H], BF16)
        ep_tmp = perm("ep_tmp", [1, H], F32)
        si_all = perm("si_all", [128, NT, H], F32)
        sia_all = perm("sia_all", [128, NT, H], BF16)
        sc_tm = perm("sc_tm", [128, NT, H], BF16)
        kv2 = perm("kv2", [128, C, 128], BF16)
        mx16 = perm("mx16", [16, 1], F32)
        nmx16 = perm("nmx16", [16, 1], F32)
        se16 = perm("se16", [16, 1], F32)
        rse16 = perm("rse16", [16, 1], F32)
        nsum = perm("nsum", [128, 2], F32)
        negmu = perm("negmu", [128, 2], F32)
        ssq = perm("ssq", [128, 2], F32)
        e2c = perm("e2c", [128, 1], F32)
        nvc = perm("nvc", [128, 1], F32)
        vpe = perm("vpe", [128, 2], F32)
        sigc = perm("sigc", [128, 2], F32)
        rsig = perm("rsig", [128, 2], F32)
        sa_t = perm("sa_t", [128, 2, H], F32)
        so_t = perm("so_t", [128, 2, H], F32)
        diag16 = perm("diag16", [16, 16], BF16)
        assert o[0] <= 25 * KiB, o[0]

        h_fm = place("h_fm", [128, HC, 512], BF16, 0)
        wo_sb = place("wo_sb", [128, C, D], BF16, 25 * KiB)
        lnh2 = [place(f"lnh2_{i}", [128, C, 512], BF16, 25 * KiB + i * 8 * KiB)
                for i in range(2)]

        wq_sb = place("wq_sb", [128, C, D], BF16, 41 * KiB)
        wk_sb = place("wk_sb", [128, C, D], BF16, 57 * KiB)
        wv_sb = place("wv_sb", [128, C, D], BF16, 73 * KiB)
        wg_sb = place("wg_sb", [128, C, HID], BF16, 41 * KiB)

        b3 = 89 * KiB
        xt1 = [place(f"xt1_{i}", [128, D], F32, b3 + i * 4 * KiB) for i in range(3)]
        sq_scr = place("sq_scr", [128, D], F32, b3 + 12 * KiB)
        ln1t = [place(f"ln1t_{i}", [128, D], BF16, b3 + 16 * KiB + i * 2 * KiB)
                for i in range(3)]
        lnh1 = [place(f"lnh1_{i}", [128, C, 128], BF16, b3 + 22 * KiB + i * 2 * KiB)
                for i in range(4)]
        qt1 = [place(f"qt1_{i}", [128, D], BF16, b3 + 30 * KiB + i * 2 * KiB)
               for i in range(2)]
        kt1 = [place(f"kt1_{i}", [128, D], BF16, b3 + 34 * KiB + i * 2 * KiB)
               for i in range(2)]
        vt1 = [place(f"vt1_{i}", [128, D], BF16, b3 + 38 * KiB + i * 2 * KiB)
               for i in range(2)]
        wu_sb = place("wu_sb", [128, C, HID], BF16, b3)

        b4 = 137 * KiB
        kfm = [place(f"kfm_{i}", [128, C, 512], BF16, b4 + i * 8 * KiB)
               for i in range(2)]
        csrc_fm = place("csrc_fm", [16, T], F32, b4 + 16 * KiB)
        exp_fm = place("exp_fm", [16, T], BF16, b4 + 32 * KiB)
        qxfm = [place(f"qxfm_{i}", [128, C, 512], BF16, b4 + i * 8 * KiB)
                for i in range(3)]
        x1t = [place(f"x1t_{i}", [128, D], F32, b4 + 24 * KiB + i * 4 * KiB)
               for i in range(2)]
        ln2t = [place(f"ln2t_{i}", [128, D], BF16, b4 + 32 * KiB + i * 2 * KiB)
                for i in range(2)]
        sq_scr2 = place("sq_scr2", [128, D], F32, b4 + 36 * KiB)
        wd_a = place("wd_a", [128, 20, D], BF16, b4)
        wd_b = place("wd_b", [128, 4, D], BF16, b4 + 40 * KiB)

        b5 = 177 * KiB
        qfm = [place(f"qfm_{i}", [128, C, 512], BF16, b5 + i * 8 * KiB)
               for i in range(2)]
        qst = [place(f"qst_{i}", [128, D], BF16, b5 + 16 * KiB + i * 2 * KiB)
               for i in range(2)]
        qsi_t = [place("qsi_0", [128, D], BF16, b5 + 20 * KiB)]
        kso_t = [place("kso_0", [128, D], BF16, b5 + 22 * KiB)]
        kst2 = [place(f"kst2_{i}", [128, D], BF16, b5 + 24 * KiB + i * 2 * KiB)
                for i in range(2)]
        kst = [place(f"kst_{i}", [128, D], BF16, b5 + i * 2 * KiB) for i in range(3)]
        vst = [place(f"vst_{i}", [128, D], BF16, b5 + 6 * KiB + i * 2 * KiB)
               for i in range(3)]
        vsc_t = [place(f"vsc_{i}", [128, D], BF16, b5 + 12 * KiB + i * 2 * KiB)
                 for i in range(2)]
        qst5 = [place(f"qst5_{i}", [128, D], BF16, b5 + 16 * KiB + i * 2 * KiB)
                for i in range(4)]
        qx_t = [place(f"qx_{i}", [128, D], BF16, b5 + 24 * KiB + i * 2 * KiB)
                for i in range(2)]
        xt5 = [place(f"xt5_{i}", [128, D], F32, b5 + i * 4 * KiB)
               for i in range(3)]
        sg_t = [place(f"sg_{i}", [128, 512], BF16, b5 + 8 * KiB + i * 1 * KiB)
                for i in range(2)]
        fout = [place(f"fout_{i}", [128, D], F32, b5 + 10 * KiB + i * 4 * KiB)
                for i in range(2)]
        c127 = place("c127", [128, 1], F32, b5 + 28 * KiB)
        assert b5 + 30 * KiB <= ARENA

        sems_list = []

        def sem(name2):
            h = ctx.enter_context(nc.semaphore(name2))
            sems_list.append(h)
            return h

        def sems(name, n):
            return [sem(f"{name}{i}") for i in range(n)]

        s_w = sem("s_w"); s_wo = sem("s_wo")
        s_x1 = sems("s_x1", 3); s_lnT = sems("s_lnT", 4)
        s_qw = sems("s_qw", 2); s_kw = sems("s_kw", 2); s_vw = sems("s_vw", 2)
        a_sq = sem("a_sq"); a_sig = sem("a_sig")
        d_vpe = sem("d_vpe"); a_sqr = sem("a_sqr"); d_ln1 = sem("d_ln1")
        d_vev = sem("d_vev"); t_q = sem("t_q"); t_k = sem("t_k"); t_v = sem("t_v")
        t_sum = sem("t_sum"); d_rows = sem("d_rows"); t_cols = sem("t_cols")
        d_blk = sem("d_blk"); s_q2 = sems("s_q2", 2); s_k2 = sems("s_k2", 2)
        s_fm2 = sems("s_fm2", 2)
        t_dA = sem("t_dA"); t_dB = sem("t_dB"); d_rec = sem("d_rec")
        d_p2 = sem("d_p2"); t_sum2 = sem("t_sum2"); s_fm3 = sems("s_fm3", 2)
        t_C = sem("t_C"); d_C = sem("d_C"); t_D = sem("t_D"); a_sa = sem("a_sa")
        d_sia = sem("d_sia"); d_mx = sem("d_mx"); a_exp = sem("a_exp")
        d_scf = sem("d_scf"); t_scT = sem("t_scT"); d_sct = sem("d_sct")
        s_k4 = sems("s_k4", 3); s_v4 = sems("s_v4", 3)
        d_vsc = sem("d_vsc"); t_kv4 = sem("t_kv4")
        d_kv2 = sem("d_kv2"); s_q5 = sems("s_q5", 4); s_x5 = sems("s_x5", 3)
        d_qx = sem("d_qx"); t_ko = sem("t_ko")
        t_qxT = sem("t_qxT"); a_qx = sem("a_qx")
        e5a = sem("e5a"); e5d = sem("e5d"); t_wo = sem("t_wo"); d_x1 = sem("d_x1")
        a_sq2 = sem("a_sq2"); d_vp2 = sem("d_vp2"); a_sr2 = sem("a_sr2")
        d_ln2 = sem("d_ln2"); s_out1 = sems("s_out1", 2); s_out2 = sems("s_out2", 2)
        s_ln2w = sems("s_ln2w", 2)
        s_wf = sem("s_wf"); s_wd = sem("s_wd"); s_lh2 = sems("s_lh2", 2)
        t_G = sem("t_G"); t_U = sem("t_U")
        s_cst = sem("s_cst")
        a_sg = sem("a_sg"); d_h = sem("d_h"); t_dn = sem("t_dn"); d_fo = sem("d_fo")
        s_dbg = sem("s_dbg"); d_self = sem("d_self")
        dsc = [0]

        P = [ctx.enter_context(nc.psum_tensor(f"P{i}", [128, 512], F32))
             for i in range(8)]
        P0, P1, P2, P3, P4, P5, P6, P7 = P

        def bfv(ps, cols):
            return ps[:, :].bitcast(BF16)[:, 0:cols]

        block = ctx.enter_context(nc.Block())

        all_sems = [h for h in sems_list]
        nc.all_engine_barrier()
        lo = min(h.num for h in all_sems)
        hi = max(h.num for h in all_sems)
        nc.gpsimd.dma_reset(range(lo, hi + 1))
        nc.gpsimd.sem_clear(range(lo, hi + 1))
        nc.all_engine_barrier()

        @block.gpsimd
        def _(g):
            g.memset(ones_r[:], 1.0).then_inc(s_cst, 1)
            g.memset(ones_c[:], 1.0).then_inc(s_cst, 1)
            g.memset(c127[:], QSC).then_inc(s_cst, 1)
            g.memset(ksb[:], 0.0).then_inc(s_cst, 1)
            g.memset(qsb[:], 0.0).then_inc(s_cst, 1)
            g.memset(skb[:], 0.0).then_inc(s_cst, 1)
            g.memset(sqb16[:], 0.0).then_inc(s_cst, 1)
            g.memset(kv2[:], 0.0).then_inc(s_cst, 1)
            g.memset(identb[:], 0.0).then_inc(s_cst, 1)
            g.wait_ge(s_cst, 9)
            g.affine_select(out=identb[:], in_=identb[:],
                            compare_op=ALU.not_equal, fill=1.0, base=0,
                            pattern=[[-1, 128]], channel_multiplier=1
                            ).then_inc(s_cst, 1)
            g.dma_start(out=wq_sb[:], in_=wq_d.rearrange("(c p) i -> p c i", p=128)
                        ).then_inc(s_w, 16)
            g.dma_start(out=wk_sb[:], in_=wk_d.rearrange("(c p) i -> p c i", p=128)
                        ).then_inc(s_w, 16)
            g.dma_start(out=wv_sb[:], in_=wv_d.rearrange("(c p) i -> p c i", p=128)
                        ).then_inc(s_w, 16)
            g.dma_start(out=b_sb[:], in_=b_d).then_inc(s_w, 16)
            g.dma_start(out=wo_sb[:], in_=wo_d.rearrange("(c p) i -> p c i", p=128)
                        ).then_inc(s_wo, 16)
            for t in range(NT):
                if t >= 3:
                    g.wait_ge(a_sq, 2 * (t - 2))
                    g.wait_ge(d_ln1, t - 2)
                g.dma_start(out=xt1[t % 3][:], in_=x_d[t * 128:(t + 1) * 128, :]
                            ).then_inc(s_x1[t % 3], 16)
                if t >= 1:
                    tt = t - 1
                    g.wait_ge(a_sig, 2 * tt + 1)
                    g.dma_start(out=q_sc[tt * 128:(tt + 1) * 128, :],
                                in_=qt1[tt % 2][:]).then_inc(s_qw[tt % 2], 16)
                    g.wait_ge(a_sig, 2 * tt + 2)
                    g.dma_start(out=k_sc[tt * 128:(tt + 1) * 128, :],
                                in_=kt1[tt % 2][:]).then_inc(s_kw[tt % 2], 16)
                    g.wait_ge(d_vev, tt + 1)
                    g.dma_start(out=v_sc[tt * 128:(tt + 1) * 128, :],
                                in_=vt1[tt % 2][:]).then_inc(s_vw[tt % 2], 16)
            tt = NT - 1
            g.wait_ge(a_sig, 2 * tt + 1)
            g.dma_start(out=q_sc[tt * 128:, :], in_=qt1[tt % 2][:]).then_inc(s_qw[tt % 2], 16)
            g.wait_ge(a_sig, 2 * tt + 2)
            g.dma_start(out=k_sc[tt * 128:, :], in_=kt1[tt % 2][:]).then_inc(s_kw[tt % 2], 16)
            g.wait_ge(d_vev, tt + 1)
            g.dma_start(out=v_sc[tt * 128:, :], in_=vt1[tt % 2][:]).then_inc(s_vw[tt % 2], 16)
            if upto < 2:
                for s_ in (s_qw + s_kw + s_vw):
                    g.wait_ge(s_, 16 * (NT // 2))
                if debug:
                    g.dma_start(out=dbg_ln1, in_=ln1t[2][:]).then_inc(s_dbg, 16)
                    g.dma_start(out=dbg_lnh, in_=lnh1[2][:]).then_inc(s_dbg, 16)
                    g.dma_start(out=dbg_st[:, 0:2], in_=ssq[:], allow_slow_non_contiguous=True).then_inc(s_dbg, 16)
                    g.dma_start(out=dbg_st[:, 2:3], in_=negmu[:], allow_slow_non_contiguous=True).then_inc(s_dbg, 16)
                    g.dma_start(out=dbg_st[:, 3:4], in_=rsig[:], allow_slow_non_contiguous=True).then_inc(s_dbg, 16)
                    g.dma_start(out=dbg_st[:, 4:6], in_=vpe[:], allow_slow_non_contiguous=True).then_inc(s_dbg, 16)
                    g.dma_start(out=dbg_st[:, 6:8], in_=sigc[:], allow_slow_non_contiguous=True).then_inc(s_dbg, 16)
                    g.wait_ge(s_dbg, 16 * 7)
                return
            for t in range(NT):
                if t >= 2:
                    g.wait_ge(d_p2, t - 1)
                g.dma_start(out=qst[t % 2][:], in_=q_sc[t * 128:(t + 1) * 128, :]
                            ).then_inc(s_q2[t % 2], 16)
                g.dma_start(out=kst2[t % 2][:],
                            in_=k_sc[t * 128:(t + 1) * 128, :]).then_inc(s_k2[t % 2], 16)
            if upto < 3:
                return
            g.dma_start(out=wg_sb[:], in_=wg_d.rearrange("(c p) h -> p c h", p=128)
                        ).then_inc(s_wf, 16)
            g.dma_start(out=wu_sb[:], in_=wu_d.rearrange("(c p) h -> p c h", p=128)
                        ).then_inc(s_wf, 16)
            if upto < 4:
                return
            if upto < 5:
                return
            for tt in range(NT):
                g.wait_ge(d_x1, tt + 1)
                g.dma_start(out=out_d[tt * 128:(tt + 1) * 128, :],
                            in_=x1t[tt % 2][:]).then_inc(s_out1[tt % 2], 16)
                g.wait_ge(d_ln2, tt + 1)
                g.dma_start(out=ln2_sc[tt * 128:(tt + 1) * 128, :],
                            in_=ln2t[tt % 2][:]).then_inc(s_ln2w[tt % 2], 16)
            if upto < 6:
                for s_ in s_out1 + s_ln2w:
                    g.wait_ge(s_, 16 * (NT // 2))
                if debug:
                    g.wait_ge(d_kv2, 1)
                    g.dma_start(out=dbg_si, in_=si_all[:]).then_inc(s_dbg, 16)
                    g.dma_start(out=dbg_sia, in_=sia_all[:]).then_inc(s_dbg, 16)
                    g.dma_start(out=dbg_sc, in_=sc_tm[:]).then_inc(s_dbg, 16)
                    g.dma_start(out=dbg_kv, in_=kv2[:]).then_inc(s_dbg, 16)
                    g.wait_ge(s_dbg, 16 * 4)
                return
            g.wait_ge(s_out1[0], 16 * (NT // 2))
            g.wait_ge(s_out1[1], 16 * (NT // 2))
            g.wait_ge(s_ln2w[0], 16 * (NT // 2))
            g.wait_ge(s_ln2w[1], 16 * (NT // 2))
            g.dma_start(out=wd_a[:],
                        in_=wd_d[0:20 * 128, :].rearrange("(c p) d -> p c d", p=128)
                        ).then_inc(s_wd, 16)
            g.dma_start(out=wd_b[:],
                        in_=wd_d[20 * 128:, :].rearrange("(c p) d -> p c d", p=128)
                        ).then_inc(s_wd, 16)
            if debug:
                g.wait_ge(d_kv2, 1)
                g.dma_start(out=dbg_si, in_=si_all[:]).then_inc(s_dbg, 16)
                g.dma_start(out=dbg_sia, in_=sia_all[:]).then_inc(s_dbg, 16)
                g.dma_start(out=dbg_sc, in_=sc_tm[:]).then_inc(s_dbg, 16)
                g.dma_start(out=dbg_kv, in_=kv2[:]).then_inc(s_dbg, 16)
                g.dma_start(out=dbg_rows[0:1, 0:D], in_=ks_row[:]).then_inc(s_dbg, 16)
                g.dma_start(out=dbg_rows[0:1, D:2 * D], in_=qs_row[:]).then_inc(s_dbg, 16)
                g.dma_start(out=dbg_rows[0:1, 2 * D:3 * D], in_=sk_row[:]).then_inc(s_dbg, 16)
                g.dma_start(out=dbg_rows[0:1, 3 * D:4 * D], in_=sq_row[:]).then_inc(s_dbg, 16)
            for gi in range(NG):
                for t4 in range(4):
                    t = gi * 4 + t4
                    g.wait_ge(d_fo, 4 * gi + t4 + 1)
                    g.dma_start(out=out_d[t * 128:(t + 1) * 128, :],
                                in_=fout[t4 % 2][:], accum_op=ALU.add
                                ).then_inc(s_out2[t4 % 2], 16)
            g.wait_ge(s_out2[0], 16 * (NT // 2))
            g.wait_ge(s_out2[1], 16 * (NT // 2))
            if debug:
                g.wait_ge(s_dbg, 16 * 8)

        @block.sync
        def _(sy):
            for t in range(NT):
                sy.wait_ge(d_ln1, t + 1)
                if t >= 4:
                    sy.wait_ge(t_v, t - 3)
                for c in range(C):
                    sy.dma_start_transpose(lnh1[t % 4][:, c, :],
                                           ln1t[t % 3][:, c * 128:(c + 1) * 128]
                                           ).then_inc(s_lnT[t % 4], 16)
            if upto < 2:
                return
            sy.wait_ge(s_qw[0], 16 * (NT // 2))
            sy.wait_ge(s_qw[1], 16 * (NT // 2))
            sy.wait_ge(s_kw[0], 16 * (NT // 2))
            sy.wait_ge(s_kw[1], 16 * (NT // 2))
            for gi in range(NG):
                if gi >= 2:
                    sy.wait_ge(t_dB, (gi - 1) * 4)
                for c in range(C):
                    sy.dma_start_transpose(
                        qfm[gi % 2][:, c, :],
                        q_sc[gi * 512:(gi + 1) * 512, c * 128:(c + 1) * 128]
                    ).then_inc(s_fm2[gi % 2], 16)
                for c in range(C):
                    sy.dma_start_transpose(
                        kfm[gi % 2][:, c, :],
                        k_sc[gi * 512:(gi + 1) * 512, c * 128:(c + 1) * 128]
                    ).then_inc(s_fm2[gi % 2], 16)
            if upto < 3:
                return
            for gi in range(NG):
                if gi == 0:
                    sy.wait_ge(t_dB, 4 * (NG - 1))
                elif gi == 1:
                    sy.wait_ge(t_sum2, NT)
                else:
                    sy.wait_ge(t_D, (gi - 1) * 4)
                    sy.wait_ge(t_C, gi - 1)
                for c in range(C):
                    sy.dma_start_transpose(
                        qfm[gi % 2][:, c, :],
                        q_sc[gi * 512:(gi + 1) * 512, c * 128:(c + 1) * 128]
                    ).then_inc(s_fm3[gi % 2], 16)
                for c in range(C):
                    sy.dma_start_transpose(
                        kfm[gi % 2][:, c, :],
                        k_sc[gi * 512:(gi + 1) * 512, c * 128:(c + 1) * 128]
                    ).then_inc(s_fm3[gi % 2], 16)
            if upto < 4:
                return
            sy.wait_ge(t_D, NT)
            sy.wait_ge(t_scT, NT)
            sy.wait_ge(s_vw[0], 16 * (NT // 2))
            sy.wait_ge(s_vw[1], 16 * (NT // 2))
            sy.wait_ge(d_p2, NT)
            for t in range(NT):
                if t >= 3:
                    sy.wait_ge(t_kv4, t - 2)
                    sy.wait_ge(d_vsc, t - 2)
                sy.dma_start(out=kst[t % 3][:], in_=k_sc[t * 128:(t + 1) * 128, :]
                             ).then_inc(s_k4[t % 3], 16)
                sy.dma_start(out=vst[t % 3][:], in_=v_sc[t * 128:(t + 1) * 128, :]
                             ).then_inc(s_v4[t % 3], 16)
                if t < 4:
                    sy.dma_start(out=qst5[t % 4][:],
                                 in_=q_sc[t * 128:(t + 1) * 128, :]
                                 ).then_inc(s_q5[t % 4], 16)
            if upto < 5:
                return
            for t in range(4, NT):
                sy.wait_ge(d_qx, t - 3)
                sy.dma_start(out=qst5[t % 4][:],
                             in_=q_sc[t * 128:(t + 1) * 128, :]
                             ).then_inc(s_q5[t % 4], 16)
            if upto < 6:
                return
            for gi in range(NG):
                if gi == 0:
                    sy.wait_ge(t_wo, NT)
                sy.wait_ge(s_ln2w[0], 16 * (2 * gi + 2))
                sy.wait_ge(s_ln2w[1], 16 * (2 * gi + 2))
                if gi >= 2:
                    sy.wait_ge(t_U, HC * (gi - 1))
                for c in range(C):
                    sy.dma_start_transpose(
                        lnh2[gi % 2][:, c, :],
                        ln2_sc[gi * 512:(gi + 1) * 512, c * 128:(c + 1) * 128]
                    ).then_inc(s_lh2[gi % 2], 16)

        @block.tensor
        def _(t_):
            t_.wait_ge(s_cst, 10)
            t_.wait_ge(s_w, 16 * 4)
            for t in range(NT):
                t_.wait_ge(s_lnT[t % 4], 16 * 8 * (t // 4 + 1))
                lnh = lnh1[t % 4]
                if t >= 1:
                    t_.wait_ge(d_vev, t)
                for c in range(C):
                    t_.matmul(P0[:], lnh[:, c, :], wq_sb[:, c, 0:512],
                              start=(c == 0), stop=False)
                    t_.matmul(P1[:], lnh[:, c, :], wq_sb[:, c, 512:1024],
                              start=(c == 0), stop=False)
                t_.matmul(P0[:], ones_r[0:1, 0:128], b_sb[0:1, 0:512],
                          start=False, stop=True)
                t_.matmul(P1[:], ones_r[0:1, 0:128], b_sb[0:1, 512:1024],
                          start=False, stop=True).then_inc(t_q, 1)
                if t >= 1:
                    t_.wait_ge(a_sig, 2 * t)
                for c in range(C):
                    t_.matmul(P2[:], lnh[:, c, :], wk_sb[:, c, 0:512],
                              start=(c == 0), stop=False)
                    t_.matmul(P3[:], lnh[:, c, :], wk_sb[:, c, 512:1024],
                              start=(c == 0), stop=False)
                t_.matmul(P2[:], ones_r[0:1, 0:128], b_sb[0:1, 1024:1536],
                          start=False, stop=True)
                t_.matmul(P3[:], ones_r[0:1, 0:128], b_sb[0:1, 1536:2048],
                          start=False, stop=True).then_inc(t_k, 1)
                t_.wait_ge(a_sig, 2 * t + 1)
                for c in range(C):
                    t_.matmul(P0[:], lnh[:, c, :], wv_sb[:, c, 0:512],
                              start=(c == 0), stop=False)
                    t_.matmul(P1[:], lnh[:, c, :], wv_sb[:, c, 512:1024],
                              start=(c == 0), stop=False)
                t_.matmul(P0[:], ones_r[0:1, 0:128], b_sb[0:1, 2048:2560],
                          start=False, stop=True)
                t_.matmul(P1[:], ones_r[0:1, 0:128], b_sb[0:1, 2560:3072],
                          start=False, stop=True).then_inc(t_v, 1)
                t_.wait_ge(a_sig, 2 * t + 2)
                t_.matmul(P4[0:1, :], ones_c[:], qt1[t % 2][:, 0:512],
                          start=(t == 0), stop=(t == NT - 1))
                t_.matmul(P5[0:1, :], ones_c[:], qt1[t % 2][:, 512:1024],
                          start=(t == 0), stop=(t == NT - 1))
                t_.matmul(P6[0:1, :], ones_c[:], kt1[t % 2][:, 0:512],
                          start=(t == 0), stop=(t == NT - 1))
                t_.matmul(P7[0:1, :], ones_c[:], kt1[t % 2][:, 512:1024],
                          start=(t == 0), stop=(t == NT - 1)).then_inc(t_sum, 1)
            if upto < 2:
                return
            pscol = bfv(P0, 32)
            t_.wait_ge(d_rows, 1)
            for c in range(C):
                t_.transpose(pscol[:, 2 * c:2 * c + 1],
                             ks_row[0:1, c * 128:(c + 1) * 128], identb[0:1, 0:1])
            for c in range(C):
                mmt = t_.transpose(pscol[:, 16 + 2 * c:16 + 2 * c + 1],
                                   qs_row[0:1, c * 128:(c + 1) * 128],
                                   identb[0:1, 0:1])
            mmt.then_inc(t_cols, 1)
            t_.wait_ge(d_blk, 1)
            psA16 = [P0[:, 0:16], P1[:, 0:16]]
            psB16 = [P2[:, 0:16], P3[:, 0:16]]
            for t in range(NT):
                gi = t // 4
                t_.wait_ge(s_fm2[gi % 2], 16 * 16 * (gi // 2 + 1))
                if t >= 2:
                    t_.wait_ge(d_rec, 2 * (t - 1))
                pa = psA16[t % 2]
                for c in range(C):
                    t_.matmul(pa[:, 2 * c:2 * c + 2],
                              qfm[gi % 2][:, c, (t % 4) * 128:(t % 4 + 1) * 128],
                              ksb[:, c, :], start=(c == 0), stop=False,
                              skip_group_check=True)
                t_.matmul(pa, ones_r[0:1, 0:128], eks_row[:],
                          start=False, stop=True, skip_group_check=True
                          ).then_inc(t_dA, 1)
                pb = psB16[t % 2]
                for c in range(C):
                    t_.matmul(pb[:, 2 * c:2 * c + 2],
                              kfm[gi % 2][:, c, (t % 4) * 128:(t % 4 + 1) * 128],
                              qsb[:, c, :], start=(c == 0), stop=False,
                              skip_group_check=True)
                t_.matmul(pb, ones_r[0:1, 0:128], eqs_row[:],
                          start=False, stop=True, skip_group_check=True
                          ).then_inc(t_dB, 1)
                t_.wait_ge(d_p2, t + 1)
                t_.matmul(P4[0:1, :], ones_c[:], qsi_t[0][:, 0:512],
                          start=(t == 0), stop=(t == NT - 1))
                t_.matmul(P5[0:1, :], ones_c[:], qsi_t[0][:, 512:1024],
                          start=(t == 0), stop=(t == NT - 1))
                t_.matmul(P6[0:1, :], ones_c[:], kso_t[0][:, 0:512],
                          start=(t == 0), stop=(t == NT - 1))
                t_.matmul(P7[0:1, :], ones_c[:], kso_t[0][:, 512:1024],
                          start=(t == 0), stop=(t == NT - 1)).then_inc(t_sum2, 1)
            t_.wait_ge(d_rows, 2)
            for c in range(C):
                t_.transpose(pscol[:, 2 * c:2 * c + 1],
                             sk_row[0:1, c * 128:(c + 1) * 128], identb[0:1, 0:1])
            for c in range(C):
                mmt = t_.transpose(pscol[:, 16 + 2 * c:16 + 2 * c + 1],
                                   sq_row[0:1, c * 128:(c + 1) * 128],
                                   identb[0:1, 0:1])
            mmt.then_inc(t_cols, 1)
            if upto < 3:
                return
            t_.wait_ge(d_blk, 2)
            t_.wait_ge(d_rec, 2 * NT)
            for gi in range(NG):
                t_.wait_ge(s_fm3[gi % 2], 16 * 16 * (gi // 2 + 1))
                if gi >= 2:
                    t_.wait_ge(d_C, gi - 1)
                pc = [P2[0:16, :], P3[0:16, :]][gi % 2]
                for c in range(C):
                    t_.matmul(pc, sqb16[:, c, :], kfm[gi % 2][:, c, :],
                              start=(c == 0), stop=False, skip_group_check=True)
                t_.matmul(pc, esq_row[:], ones_r[:], start=False, stop=True,
                          skip_group_check=True).then_inc(t_C, 1)
                for t4 in range(4):
                    t = gi * 4 + t4
                    if t >= 2:
                        t_.wait_ge(a_sa, t - 1)
                    pd = psA16[t % 2]
                    for c in range(C):
                        t_.matmul(pd[:, 2 * c:2 * c + 2],
                                  qfm[gi % 2][:, c, t4 * 128:(t4 + 1) * 128],
                                  skb[:, c, :], start=(c == 0), stop=False,
                                  skip_group_check=True)
                    t_.matmul(pd, ones_r[0:1, 0:128], esk_row[:],
                              start=False, stop=True, skip_group_check=True
                              ).then_inc(t_D, 1)
            t_.wait_ge(d_scf, 1)
            psT = [bfv(P4, 16), bfv(P5, 16)]
            for i in range(NT):
                if i >= 2:
                    t_.wait_ge(d_sct, i - 1)
                t_.transpose(psT[i % 2][:, :],
                             exp_fm[0:16, i * 128:(i + 1) * 128],
                             identb[0:16, 0:16]).then_inc(t_scT, 1)
            if upto < 4:
                return
            kvps = [P2, P3]
            for t in range(NT):
                t_.wait_ge(s_k4[t % 3], 16 * (t // 3 + 1))
                t_.wait_ge(d_vsc, t + 1)
                mm = None
                for c in range(C):
                    mm = t_.matmul(
                        kvps[c // 4][:, (c % 4) * 128:(c % 4 + 1) * 128],
                        kst[t % 3][:, c * 128:(c + 1) * 128],
                        vsc_t[t % 2][:, c * 128:(c + 1) * 128],
                        start=(t == 0 and c % 4 == 0),
                        stop=(t == NT - 1 and c % 4 == 3),
                        skip_group_check=True)
                mm.then_inc(t_kv4, 1)
            if upto < 5:
                return
            t_.wait_ge(d_kv2, 1)
            t_.wait_ge(s_wo, 16)
            psO = [P0, P1]
            pqx = [bfv(P2, 1024), bfv(P3, 1024)]
            for gi in range(NG):
                for t4 in range(4):
                    t = gi * 4 + t4
                    t_.wait_ge(d_qx, t + 1)
                    if t >= 2:
                        t_.wait_ge(a_qx, t - 1)
                    mmq = None
                    for c in range(C):
                        mmq = t_.transpose(pqx[t % 2][:, c * 128:(c + 1) * 128],
                                           qx_t[t % 2][:, c * 128:(c + 1) * 128],
                                           identb[:, :])
                    mmq.then_inc(t_qxT, 1)
                t_.wait_ge(a_qx, 4 * (gi + 1))
                for c in range(C):
                    idx = gi * C + c
                    if idx >= 2:
                        prev = idx - 2
                        if prev % 2 == 0:
                            t_.wait_ge(e5a, prev // 2 + 1)
                        else:
                            t_.wait_ge(e5d, prev // 2 + 1)
                    t_.matmul(psO[c % 2][:], kv2[:, c, :], qxfm[gi % 3][:, c, :],
                              start=True, stop=True).then_inc(t_ko, 1)
                t_.wait_ge(e5a, 4 * (gi + 1))
                t_.wait_ge(e5d, 4 * (gi + 1))
                for t4 in range(4):
                    t = gi * 4 + t4
                    if t >= 2:
                        t_.wait_ge(d_x1, t - 1)
                    pw0, pw1 = [(P4, P5), (P6, P7)][t % 2]
                    for c in range(C):
                        t_.matmul(pw0[:], qxfm[gi % 3][:, c, t4 * 128:(t4 + 1) * 128],
                                  wo_sb[:, c, 0:512], start=(c == 0), stop=False)
                        t_.matmul(pw1[:], qxfm[gi % 3][:, c, t4 * 128:(t4 + 1) * 128],
                                  wo_sb[:, c, 512:1024], start=(c == 0), stop=False)
                    t_.matmul(pw0[:], ones_r[0:1, 0:128], b_sb[0:1, 3072:3584],
                              start=False, stop=True)
                    t_.matmul(pw1[:], ones_r[0:1, 0:128], b_sb[0:1, 3584:4096],
                              start=False, stop=True).then_inc(t_wo, 1)
            if upto < 6:
                return
            t_.wait_ge(d_x1, NT)
            t_.wait_ge(s_wf, 32)
            psG = [P0, P1]
            psU = [P2, P3]
            for gi in range(NG):
                t_.wait_ge(s_lh2[gi % 2], 16 * 8 * (gi // 2 + 1))
                for hc in range(HC):
                    idx = gi * HC + hc
                    if idx >= 2:
                        t_.wait_ge(a_sg, idx - 1)
                    mmg = None
                    for j in range(C):
                        mmg = t_.matmul(psG[hc % 2][:],
                                        wg_sb[:, j, hc * 128:(hc + 1) * 128],
                                        lnh2[gi % 2][:, j, :],
                                        start=(j == 0), stop=(j == C - 1))
                    mmg.then_inc(t_G, 1)
                    if idx >= 2:
                        t_.wait_ge(d_h, idx - 1)
                    mmu = None
                    for j in range(C):
                        mmu = t_.matmul(psU[hc % 2][:],
                                        wu_sb[:, j, hc * 128:(hc + 1) * 128],
                                        lnh2[gi % 2][:, j, :],
                                        start=(j == 0), stop=(j == C - 1))
                    mmu.then_inc(t_U, 1)
                t_.wait_ge(d_h, HC * (gi + 1))
                if gi == 0:
                    t_.wait_ge(s_wd, 32)
                for t4 in range(4):
                    gt = gi * 4 + t4
                    if gt >= 2:
                        t_.wait_ge(d_fo, gt - 1)
                    pd0, pd1 = [(P4, P5), (P6, P7)][t4 % 2]
                    def wd_at(hc):
                        return wd_a[:, hc, :] if hc < 20 else wd_b[:, hc - 20, :]
                    for hc in range(HC):
                        t_.matmul(pd0[:], h_fm[:, hc, t4 * 128:(t4 + 1) * 128],
                                  wd_at(hc)[:, 0:512], start=(hc == 0),
                                  stop=(hc == HC - 1))
                    mmd = None
                    for hc in range(HC):
                        mmd = t_.matmul(pd1[:], h_fm[:, hc, t4 * 128:(t4 + 1) * 128],
                                        wd_at(hc)[:, 512:1024], start=(hc == 0),
                                        stop=(hc == HC - 1))
                    mmd.then_inc(t_dn, 1)

        @block.scalar
        def _(s):
            s.wait_ge(s_cst, 10)
            for t in range(NT):
                s.wait_ge(s_x1[t % 3], 16 * (t // 3 + 1))
                if t >= 2:
                    s.wait_ge(d_vpe, t - 1)
                s.activation(sq_scr[:], xt1[t % 3][:], AF.Square,
                             accum_out=ssq[:, t % 2:t % 2 + 1]).then_inc(a_sq, 1)
                s.activation(sq_scr[:], xt1[t % 3][:], AF.Copy,
                             accum_out=nsum[:, t % 2:t % 2 + 1]).then_inc(a_sq, 1)
                s.wait_ge(d_vpe, t + 1)
                if t >= 2:
                    s.wait_ge(d_ln1, t - 1)
                s.activation(sigc[:, t % 2:t % 2 + 1], vpe[:, t % 2:t % 2 + 1],
                             AF.Sqrt).then_inc(a_sqr, 1)
                s.wait_ge(t_q, t + 1)
                if t >= 2:
                    s.wait_ge(t_sum, t - 1)
                    s.wait_ge(s_qw[t % 2], 16 * (t // 2))
                s.activation(qt1[t % 2][:, 0:512], P0[:], AF.Sigmoid)
                s.activation(qt1[t % 2][:, 512:1024], P1[:], AF.Sigmoid
                             ).then_inc(a_sig, 1)
                s.wait_ge(t_k, t + 1)
                if t >= 2:
                    s.wait_ge(s_kw[t % 2], 16 * (t // 2))
                s.activation(kt1[t % 2][:, 0:512], P2[:], AF.Sigmoid)
                s.activation(kt1[t % 2][:, 512:1024], P3[:], AF.Sigmoid
                             ).then_inc(a_sig, 1)
            if upto < 3:
                return
            psA16 = [P0[:, 0:16], P1[:, 0:16]]
            for t in range(NT):
                s.wait_ge(t_D, t + 1)
                if t >= 2:
                    s.wait_ge(d_sia, t - 1)
                s.activation(sa_t[:, t % 2, :], psA16[t % 2], AF.Sigmoid
                             ).then_inc(a_sa, 1)
            s.wait_ge(d_mx, 1)
            s.activation(exp_fm[:], csrc_fm[:], AF.Exp, bias=nmx16[:],
                         accum_out=se16[:]).then_inc(a_exp, 1)
            if upto < 5:
                return
            s.wait_ge(d_kv2, 1)
            for t in range(3):
                s.dma_start(out=xt5[t % 3][:], in_=x_d[t * 128:(t + 1) * 128, :]
                            ).then_inc(s_x5[t % 3], 16)
            pqx = [bfv(P2, 1024), bfv(P3, 1024)]
            for gi in range(NG):
                for t4 in range(4):
                    t = gi * 4 + t4
                    s.wait_ge(t_qxT, t + 1)
                    s.activation(qxfm[gi % 3][:, 0:4, t4 * 128:(t4 + 1) * 128],
                                 pqx[t % 2][:, 0:512].rearrange(
                                     "p (c x) -> p c x", c=4), AF.Copy)
                    s.activation(qxfm[gi % 3][:, 4:8, t4 * 128:(t4 + 1) * 128],
                                 pqx[t % 2][:, 512:1024].rearrange(
                                     "p (c x) -> p c x", c=4), AF.Copy
                                 ).then_inc(a_qx, 1)
                for c in range(0, C, 2):
                    s.wait_ge(t_ko, gi * C + c + 1)
                    s.activation(qxfm[gi % 3][:, c, :], [P0, P1][c % 2][:],
                                 AF.Copy).then_inc(e5a, 1)
                for t4 in range(4):
                    t = gi * 4 + t4
                    s.wait_ge(d_x1, t + 1)
                    if t >= 2:
                        s.wait_ge(d_vp2, t - 1)
                    s.activation(sq_scr2[:], x1t[t % 2][:], AF.Square,
                                 accum_out=ssq[:, t % 2:t % 2 + 1]).then_inc(a_sq2, 1)
                    s.activation(sq_scr2[:], x1t[t % 2][:], AF.Copy,
                                 accum_out=nsum[:, t % 2:t % 2 + 1]).then_inc(a_sq2, 1)
                    if t + 3 < NT:
                        s.dma_start(out=xt5[(t + 3) % 3][:],
                                    in_=x_d[(t + 3) * 128:(t + 4) * 128, :]
                                    ).then_inc(s_x5[t % 3], 16)
                    s.wait_ge(d_vp2, t + 1)
                    if t >= 2:
                        s.wait_ge(d_ln2, t - 1)
                    s.activation(sigc[:, t % 2:t % 2 + 1], vpe[:, t % 2:t % 2 + 1],
                                 AF.Sqrt).then_inc(a_sr2, 1)
            if upto < 6:
                return
            for gi in range(NG):
                for hc in range(HC):
                    idx = gi * HC + hc
                    s.wait_ge(t_G, idx + 1)
                    if idx >= 2:
                        s.wait_ge(d_h, idx - 1)
                    s.activation(sg_t[hc % 2][:], [P0, P1][hc % 2][:],
                                 AF.Silu, scale=c127[:]).then_inc(a_sg, 1)

        @block.vector
        def _(v):
            v.wait_ge(s_cst, 10)
            for t in range(NT + 1):
                if t >= 1:
                    tt = t - 1
                    v.wait_ge(a_sqr, tt + 1)
                    dsc[0] += 1
                    v.reciprocal(rsig[:, tt % 2:tt % 2 + 1],
                                 sigc[:, tt % 2:tt % 2 + 1]).then_inc(d_self, 1)
                    v.wait_ge(d_self, dsc[0])
                    if tt >= 3:
                        v.wait_ge(s_lnT[(tt - 3) % 4], 16 * 8 * ((tt - 3) // 4 + 1))
                    v.tensor_scalar(ln1t[tt % 3][:], xt1[tt % 3][:],
                                    negmu[:, tt % 2:tt % 2 + 1],
                                    rsig[:, tt % 2:tt % 2 + 1],
                                    ALU.add, ALU.mult).then_inc(d_ln1, 1)
                if t < NT:
                    if t >= 1:
                        tt = t - 1
                        v.wait_ge(t_v, tt + 1)
                        if tt >= 2:
                            v.wait_ge(s_vw[tt % 2], 16 * (tt // 2))
                        v.tensor_copy(vt1[tt % 2][:, 0:512], P0[:])
                        v.tensor_copy(vt1[tt % 2][:, 512:1024], P1[:]
                                      ).then_inc(d_vev, 1)
                    v.wait_ge(a_sq, 2 * t + 2)
                    v.tensor_scalar(negmu[:, t % 2:t % 2 + 1],
                                    nsum[:, t % 2:t % 2 + 1], -1.0 / D,
                                    None, ALU.mult)
                    dsc[0] += 1
                    v.tensor_scalar(e2c[:], ssq[:, t % 2:t % 2 + 1], 1.0 / D,
                                    None, ALU.mult).then_inc(d_self, 1)
                    v.wait_ge(d_self, dsc[0])
                    dsc[0] += 1
                    v.scalar_tensor_tensor(nvc[:], negmu[:, t % 2:t % 2 + 1],
                                           negmu[:, t % 2:t % 2 + 1], e2c[:],
                                           ALU.mult, ALU.subtract
                                           ).then_inc(d_self, 1)
                    v.wait_ge(d_self, dsc[0])
                    v.tensor_scalar(vpe[:, t % 2:t % 2 + 1], nvc[:], -1.0, LNEPS,
                                    ALU.mult, ALU.add).then_inc(d_vpe, 1)
            tt = NT - 1
            v.wait_ge(t_v, tt + 1)
            v.tensor_copy(vt1[tt % 2][:, 0:512], P0[:])
            v.tensor_copy(vt1[tt % 2][:, 512:1024], P1[:]).then_inc(d_vev, 1)
            if upto < 2:
                return
            v.wait_ge(t_sum, NT)
            v.tensor_copy(qs_row[0:1, 0:512], P4[0:1, :])
            v.tensor_copy(qs_row[0:1, 512:1024], P5[0:1, :])
            v.tensor_copy(ks_row[0:1, 0:512], P6[0:1, :])
            v.tensor_copy(ks_row[0:1, 512:1024], P7[0:1, :]).then_inc(d_rows, 1)
            pscol = bfv(P0, 32)
            v.wait_ge(t_cols, 1)
            for c in range(C):
                v.tensor_scalar(ksb[0:64, c, 0:1], pscol[0:64, 2 * c:2 * c + 1],
                                EPS, None, ALU.add)
                v.tensor_scalar(ksb[64:128, c, 1:2], pscol[64:128, 2 * c:2 * c + 1],
                                EPS, None, ALU.add)
                v.tensor_scalar(qsb[0:64, c, 0:1],
                                pscol[0:64, 16 + 2 * c:16 + 2 * c + 1], EPS,
                                None, ALU.add)
                v.tensor_scalar(qsb[64:128, c, 1:2],
                                pscol[64:128, 16 + 2 * c:16 + 2 * c + 1],
                                EPS, None, ALU.add)
            v.reduce_sum(ep_tmp[0:1, :].unsqueeze(2),
                         ks_row[0:1, :].rearrange("p (h d) -> p h d", h=H),
                         axis=AX.X, opt_input=False, opt_output=False
                         ).then_inc(d_self, 1)
            dsc[0] += 1
            v.wait_ge(d_self, dsc[0])
            dsc[0] += 1
            v.tensor_scalar(ep_tmp[:], ep_tmp[:], 64.0 * EPS, EPS, ALU.add,
                            ALU.mult).then_inc(d_self, 1)
            v.wait_ge(d_self, dsc[0])
            v.tensor_copy(eks_row[:], ep_tmp[:])
            v.reduce_sum(ep_tmp[0:1, :].unsqueeze(2),
                         qs_row[0:1, :].rearrange("p (h d) -> p h d", h=H),
                         axis=AX.X, opt_input=False, opt_output=False
                         ).then_inc(d_self, 1)
            dsc[0] += 1
            v.wait_ge(d_self, dsc[0])
            dsc[0] += 1
            v.tensor_scalar(ep_tmp[:], ep_tmp[:], 64.0 * EPS, EPS, ALU.add,
                            ALU.mult).then_inc(d_self, 1)
            v.wait_ge(d_self, dsc[0])
            v.tensor_copy(eqs_row[:], ep_tmp[:]).then_inc(d_blk, 1)
            psA16 = [P0[:, 0:16], P1[:, 0:16]]
            psB16 = [P2[:, 0:16], P3[:, 0:16]]
            for t in range(NT):
                v.wait_ge(t_dA, t + 1)
                v.reciprocal(si_all[:, t, :], psA16[t % 2])
                v.wait_ge(t_dB, t + 1)
                dsc[0] += 1
                v.reciprocal(so_t[:, t % 2, :], psB16[t % 2]
                             ).then_inc(d_self, 1)
                v.wait_ge(d_self, dsc[0])
                v.engine_nop().then_inc(d_rec, 2)
                v.wait_ge(s_q2[t % 2], 16 * (t // 2 + 1))
                v.wait_ge(s_k2[t % 2], 16 * (t // 2 + 1))
                if t >= 1:
                    v.wait_ge(t_sum2, t)
                v.tensor_tensor(
                    qsi_t[0][:, :].rearrange("p (h d) -> p h d", h=H),
                    qst[t % 2][:, :].rearrange("p (h d) -> p h d", h=H),
                    si_all[:, t, :].unsqueeze(2).broadcast_to([128, H, HD]),
                    ALU.mult)
                v.tensor_tensor(
                    kso_t[0][:, :].rearrange("p (h d) -> p h d", h=H),
                    kst2[t % 2][:, :].rearrange("p (h d) -> p h d", h=H),
                    so_t[:, t % 2, :].unsqueeze(2).broadcast_to([128, H, HD]),
                    ALU.mult).then_inc(d_p2, 1)
            v.wait_ge(t_sum2, NT)
            v.tensor_copy(sq_row[0:1, 0:512], P4[0:1, :])
            v.tensor_copy(sq_row[0:1, 512:1024], P5[0:1, :])
            v.tensor_copy(sk_row[0:1, 0:512], P6[0:1, :])
            v.tensor_copy(sk_row[0:1, 512:1024], P7[0:1, :]).then_inc(d_rows, 1)
            v.wait_ge(t_cols, 2)
            for c in range(C):
                v.tensor_scalar(skb[0:64, c, 0:1], pscol[0:64, 2 * c:2 * c + 1],
                                EPS, None, ALU.add)
                v.tensor_scalar(skb[64:128, c, 1:2], pscol[64:128, 2 * c:2 * c + 1],
                                EPS, None, ALU.add)
                v.tensor_scalar(sqb16[0:64, c, 2 * c:2 * c + 1],
                                pscol[0:64, 16 + 2 * c:16 + 2 * c + 1], EPS,
                                None, ALU.add)
                v.tensor_scalar(sqb16[64:128, c, 2 * c + 1:2 * c + 2],
                                pscol[64:128, 16 + 2 * c:16 + 2 * c + 1],
                                EPS, None, ALU.add)
            v.reduce_sum(ep_tmp[0:1, :].unsqueeze(2),
                         sk_row[0:1, :].rearrange("p (h d) -> p h d", h=H),
                         axis=AX.X, opt_input=False, opt_output=False
                         ).then_inc(d_self, 1)
            dsc[0] += 1
            v.wait_ge(d_self, dsc[0])
            dsc[0] += 1
            v.tensor_scalar(ep_tmp[:], ep_tmp[:], 64.0 * EPS, EPS, ALU.add,
                            ALU.mult).then_inc(d_self, 1)
            v.wait_ge(d_self, dsc[0])
            v.tensor_copy(esk_row[:], ep_tmp[:])
            v.reduce_sum(ep_tmp[0:1, :].unsqueeze(2),
                         sq_row[0:1, :].rearrange("p (h d) -> p h d", h=H),
                         axis=AX.X, opt_input=False, opt_output=False
                         ).then_inc(d_self, 1)
            dsc[0] += 1
            v.wait_ge(d_self, dsc[0])
            dsc[0] += 1
            v.tensor_scalar(ep_tmp[:], ep_tmp[:], 64.0 * EPS, EPS, ALU.add,
                            ALU.mult).then_inc(d_self, 1)
            v.wait_ge(d_self, dsc[0])
            v.tensor_copy(esq_row[:], ep_tmp[:]).then_inc(d_blk, 1)
            if upto < 3:
                return
            for gi in range(NG):
                v.wait_ge(t_C, gi + 1)
                pc = [P2[0:16, :], P3[0:16, :]][gi % 2]
                v.tensor_scalar(csrc_fm[:, gi * 512:(gi + 1) * 512], pc,
                                1.0, -1.0, ALU.min, ALU.max).then_inc(d_C, 1)
                for t4 in range(4):
                    t = gi * 4 + t4
                    v.wait_ge(a_sa, t + 1)
                    v.tensor_tensor(sia_all[:, t, :], si_all[:, t, :],
                                    sa_t[:, t % 2, :], ALU.mult).then_inc(d_sia, 1)
            v.wait_ge(d_C, NG)
            v.reduce_max(mx16[:], csrc_fm[:], axis=AX.X, opt_input=False,
                         opt_output=False).then_inc(d_self, 1)
            dsc[0] += 1
            v.wait_ge(d_self, dsc[0])
            v.tensor_scalar(nmx16[:], mx16[:], -1.0, None, ALU.mult).then_inc(d_mx, 1)
            v.wait_ge(a_exp, 1)
            dsc[0] += 1
            v.reciprocal(rse16[:], se16[:]).then_inc(d_self, 1)
            v.wait_ge(d_self, dsc[0])
            v.tensor_scalar(exp_fm[:], exp_fm[:], rse16[:], SFL,
                            ALU.mult, ALU.mult).then_inc(d_scf, 1)
            psT = [bfv(P4, 16), bfv(P5, 16)]
            for i in range(NT):
                v.wait_ge(t_scT, i + 1)
                v.tensor_copy(sc_tm[:, i, :], psT[i % 2][:, :]).then_inc(d_sct, 1)
            if upto < 4:
                return
            for t in range(NT):
                v.wait_ge(s_v4[t % 3], 16 * (t // 3 + 1))
                if t >= 2:
                    v.wait_ge(t_kv4, t - 1)
                v.tensor_tensor(
                    vsc_t[t % 2][:, :].rearrange("p (h d) -> p h d", h=H),
                    vst[t % 3][:, :].rearrange("p (h d) -> p h d", h=H),
                    sc_tm[:, t, :].unsqueeze(2).broadcast_to([128, H, HD]),
                    ALU.mult).then_inc(d_vsc, 1)
            v.wait_ge(t_kv4, NT)
            for c in range(C):
                ps = [P2, P3][c // 4]
                off = (c % 4) * 128
                v.tensor_copy(kv2[0:64, c, 0:64], ps[0:64, off:off + 64])
                if c == C - 1:
                    v.tensor_copy(kv2[64:128, c, 64:128],
                                  ps[64:128, off + 64:off + 128]).then_inc(d_kv2, 1)
                else:
                    v.tensor_copy(kv2[64:128, c, 64:128],
                                  ps[64:128, off + 64:off + 128])
            if upto < 5:
                return
            for gi in range(NG):
                for t4 in range(4):
                    t = gi * 4 + t4
                    v.wait_ge(s_q5[t % 4], 16 * (t // 4 + 1))
                    if t >= 2:
                        v.wait_ge(t_qxT, t - 1)
                    v.tensor_tensor(
                        qx_t[t % 2][:, :].rearrange("p (h d) -> p h d", h=H),
                        qst5[t % 4][:, :].rearrange("p (h d) -> p h d", h=H),
                        sia_all[:, t, :].unsqueeze(2).broadcast_to([128, H, HD]),
                        ALU.mult).then_inc(d_qx, 1)
                for c in range(1, C, 2):
                    v.wait_ge(t_ko, gi * C + c + 1)
                    v.tensor_copy(qxfm[gi % 3][:, c, :], [P0, P1][c % 2][:]
                                  ).then_inc(e5d, 1)
                for t4 in range(4):
                    t = gi * 4 + t4
                    v.wait_ge(t_wo, t + 1)
                    v.wait_ge(s_x5[t % 3], 16 * (t // 3 + 1))
                    if t >= 2:
                        v.wait_ge(s_out1[t % 2], 16 * (t // 2))
                        v.wait_ge(a_sq2, 2 * (t - 1))
                    pw0, pw1 = [(P4, P5), (P6, P7)][t % 2]
                    v.scalar_tensor_tensor(x1t[t % 2][:, 0:512], pw0[:], 1.0,
                                           xt5[t % 3][:, 0:512], ALU.mult, ALU.add)
                    v.scalar_tensor_tensor(x1t[t % 2][:, 512:1024], pw1[:], 1.0,
                                           xt5[t % 3][:, 512:1024], ALU.mult,
                                           ALU.add).then_inc(d_x1, 1)
                    v.wait_ge(a_sq2, 2 * t + 2)
                    v.tensor_scalar(negmu[:, t % 2:t % 2 + 1],
                                    nsum[:, t % 2:t % 2 + 1], -1.0 / D,
                                    None, ALU.mult)
                    dsc[0] += 1
                    v.tensor_scalar(e2c[:], ssq[:, t % 2:t % 2 + 1], 1.0 / D,
                                    None, ALU.mult).then_inc(d_self, 1)
                    v.wait_ge(d_self, dsc[0])
                    dsc[0] += 1
                    v.scalar_tensor_tensor(nvc[:], negmu[:, t % 2:t % 2 + 1],
                                           negmu[:, t % 2:t % 2 + 1], e2c[:],
                                           ALU.mult, ALU.subtract
                                           ).then_inc(d_self, 1)
                    v.wait_ge(d_self, dsc[0])
                    v.tensor_scalar(vpe[:, t % 2:t % 2 + 1], nvc[:], -1.0, LNEPS,
                                    ALU.mult, ALU.add).then_inc(d_vp2, 1)
                    if t >= 1:
                        tt = t - 1
                        v.wait_ge(a_sr2, tt + 1)
                        dsc[0] += 1
                        v.reciprocal(rsig[:, tt % 2:tt % 2 + 1],
                                     sigc[:, tt % 2:tt % 2 + 1]).then_inc(d_self, 1)
                        v.wait_ge(d_self, dsc[0])
                        if tt >= 2:
                            v.wait_ge(s_ln2w[tt % 2], 16 * (tt // 2))
                        v.tensor_scalar(ln2t[tt % 2][:], x1t[tt % 2][:],
                                        negmu[:, tt % 2:tt % 2 + 1],
                                        rsig[:, tt % 2:tt % 2 + 1],
                                        ALU.add, ALU.mult).then_inc(d_ln2, 1)
                if gi == NG - 1:
                    tt = NT - 1
                    v.wait_ge(a_sr2, tt + 1)
                    dsc[0] += 1
                    v.reciprocal(rsig[:, tt % 2:tt % 2 + 1],
                                 sigc[:, tt % 2:tt % 2 + 1]).then_inc(d_self, 1)
                    v.wait_ge(d_self, dsc[0])
                    v.wait_ge(s_ln2w[tt % 2], 16 * (tt // 2))
                    v.tensor_scalar(ln2t[tt % 2][:], x1t[tt % 2][:],
                                    negmu[:, tt % 2:tt % 2 + 1],
                                    rsig[:, tt % 2:tt % 2 + 1],
                                    ALU.add, ALU.mult).then_inc(d_ln2, 1)
            if upto < 6:
                return
            for gi in range(NG):
                for hc in range(HC):
                    idx = gi * HC + hc
                    v.wait_ge(a_sg, idx + 1)
                    v.wait_ge(t_U, idx + 1)
                    if hc == 0 and gi >= 1:
                        v.wait_ge(t_dn, 4 * gi)
                    v.tensor_tensor(h_fm[:, hc, :], sg_t[hc % 2][:],
                                    [P2, P3][hc % 2][:], ALU.mult).then_inc(d_h, 1)
                for t4 in range(4):
                    gt = gi * 4 + t4
                    v.wait_ge(t_dn, gt + 1)
                    if gt >= 2:
                        v.wait_ge(s_out2[gt % 2], 16 * (gt // 2))
                    pd0, pd1 = [(P4, P5), (P6, P7)][t4 % 2]
                    v.tensor_scalar(fout[t4 % 2][:, 0:512], pd0[:], DSC, None,
                                    ALU.mult)
                    v.tensor_scalar(fout[t4 % 2][:, 512:1024], pd1[:], DSC, None,
                                    ALU.mult).then_inc(d_fo, 1)

    return nc


def kernel(L0_attn_x, L0_attn_Wq, L0_attn_bq, L0_attn_Wk, L0_attn_bk,
           L0_attn_Wv, L0_attn_bv, L0_attn_Wo, L0_attn_bo,
           L0_ffn_w_gate, L0_ffn_w_up, L0_ffn_w_down):
    x = np.ascontiguousarray(np.asarray(L0_attn_x, dtype=np.float32))
    wqT = np.ascontiguousarray(np.asarray(L0_attn_Wq, np.float32).T)
    wkT = np.ascontiguousarray(np.asarray(L0_attn_Wk, np.float32).T)
    wvT = np.ascontiguousarray(np.asarray(L0_attn_Wv, np.float32).T)
    woT = np.ascontiguousarray(np.asarray(L0_attn_Wo, np.float32).T)
    bcat = np.ascontiguousarray(np.concatenate([
        np.asarray(L0_attn_bq, np.float32), np.asarray(L0_attn_bk, np.float32),
        np.asarray(L0_attn_bv, np.float32), np.asarray(L0_attn_bo, np.float32),
    ])[None, :])
    wgT = np.ascontiguousarray(np.asarray(L0_ffn_w_gate, np.int32).T)
    wuT = np.ascontiguousarray(np.asarray(L0_ffn_w_up, np.int32).T)
    wdT = np.ascontiguousarray(np.asarray(L0_ffn_w_down, np.int32).T)

    nc = build_program()
    in_maps = []
    for c in range(8):
        in_maps.append(dict(x=np.ascontiguousarray(x[c]), wqT=wqT, wkT=wkT,
                            wvT=wvT, woT=woT, bcat=bcat, wgT=wgT, wuT=wuT,
                            wdT=wdT))
    res = run_bass_kernel_spmd(nc, in_maps, core_ids=list(range(8)))
    out = np.stack([np.asarray(r["out"]) for r in res.results], axis=0)
    return out.astype(np.float32)


# version 06
# speedup vs baseline: 1.1303x; 1.1110x over previous
import contextlib
import numpy as np
import ml_dtypes
import concourse.bass as bass
import concourse.mybir as mybir
from concourse.bass_utils import run_bass_kernel_spmd

F32 = mybir.dt.float32
BF16 = mybir.dt.bfloat16
I32 = mybir.dt.int32
ALU = mybir.AluOpType
AF = mybir.ActivationFunctionType
AX = mybir.AxisListType
BF = ml_dtypes.bfloat16

T = 4096
D = 1024
NT = 32
NG = 8
C = 8
H = 16
HD = 64
HID = 3072
HC = 24
EPS = 1e-6
LNEPS = 1e-5
QSC = 1.0 / 127.0
DSC = 1.0 / (127.0 * 127.0)
SFL = 4096.0

KiB = 1024
ARENA = 207 * KiB


def build_program(debug=False, upto=6):
    nc = bass.Bass("TRN2", target_bir_lowering=False,
                   detect_race_conditions=False)

    x_d = nc.dram_tensor("x", [T, D], F32, kind="ExternalInput").ap()
    wq_d = nc.dram_tensor("wqT", [D, D], F32, kind="ExternalInput").ap()
    wk_d = nc.dram_tensor("wkT", [D, D], F32, kind="ExternalInput").ap()
    wv_d = nc.dram_tensor("wvT", [D, D], F32, kind="ExternalInput").ap()
    wo_d = nc.dram_tensor("woT", [D, D], F32, kind="ExternalInput").ap()
    b_d = nc.dram_tensor("bcat", [1, 4 * D], F32, kind="ExternalInput").ap()
    wg_d = nc.dram_tensor("wgT", [D, HID], I32, kind="ExternalInput").ap()
    wu_d = nc.dram_tensor("wuT", [D, HID], I32, kind="ExternalInput").ap()
    wd_d = nc.dram_tensor("wdT", [HID, D], I32, kind="ExternalInput").ap()
    out_d = nc.dram_tensor("out", [T, D], F32, kind="ExternalOutput").ap()

    skind = "ExternalOutput" if debug else "Internal"
    q_sc = nc.dram_tensor("q_sc", [T, D], BF16, kind=skind).ap()
    k_sc = nc.dram_tensor("k_sc", [T, D], BF16, kind=skind).ap()
    v_sc = nc.dram_tensor("v_sc", [T, D], BF16, kind=skind).ap()
    ln2_sc = nc.dram_tensor("ln2_sc", [T, D], BF16, kind=skind).ap()
    if debug:
        dbg_si = nc.dram_tensor("dbg_si", [128, NT, H], F32, kind="ExternalOutput").ap()
        dbg_sia = nc.dram_tensor("dbg_sia", [128, NT, H], BF16, kind="ExternalOutput").ap()
        dbg_sc = nc.dram_tensor("dbg_sc", [128, NT, H], BF16, kind="ExternalOutput").ap()
        dbg_kv = nc.dram_tensor("dbg_kv", [128, C, 128], BF16, kind="ExternalOutput").ap()
        dbg_rows = nc.dram_tensor("dbg_rows", [1, 4 * D], BF16, kind="ExternalOutput").ap()
        dbg_ln1 = nc.dram_tensor("dbg_ln1", [128, D], BF16, kind="ExternalOutput").ap()
        dbg_st = nc.dram_tensor("dbg_st", [128, 8], F32, kind="ExternalOutput").ap()
        dbg_lnh = nc.dram_tensor("dbg_lnh", [128, C, 128], BF16, kind="ExternalOutput").ap()

    ctx = contextlib.ExitStack()
    with ctx:
        arena_h = ctx.enter_context(
            nc.sbuf_tensor("arena", [128, ARENA], mybir.dt.uint8))
        arena_base = nc.lookup_mloc(arena_h).addr

        def place(name, shape, dtype, off):
            assert off + int(np.prod(shape[1:])) * mybir.dt.size(dtype) <= ARENA, name
            nc.dbg_offsets = getattr(nc, "dbg_offsets", {})
            nc.dbg_offsets[name] = (off, tuple(shape), dtype)
            h = nc.virtual_sbuf_tensor(name, shape, dtype)
            nc._tensor(h.name, h.shape, h.dtype, type="SB")
            mloc = nc.lookup_mloc(h)
            mloc.allocated = True
            mloc.addr = arena_base + off
            return h

        o = [0]
        def perm(name, shape, dtype):
            h = place(name, shape, dtype, o[0])
            o[0] += (int(np.prod(shape[1:])) * mybir.dt.size(dtype) + 31) // 32 * 32
            return h

        ones_r = perm("ones_r", [1, 512], BF16)
        ones_c = perm("ones_c", [128, 1], BF16)
        identb = perm("identb", [128, 128], BF16)
        b_sb = perm("b_sb", [1, 4 * D], BF16)
        ks_row = perm("ks_row", [1, D], BF16)
        qs_row = perm("qs_row", [1, D], BF16)
        sk_row = perm("sk_row", [1, D], BF16)
        sq_row = perm("sq_row", [1, D], BF16)
        ksb = perm("ksb", [128, C, 2], BF16)
        qsb = perm("qsb", [128, C, 2], BF16)
        skb = perm("skb", [128, C, 2], BF16)
        sqb16 = perm("sqb16", [128, C, H], BF16)
        eks_row = perm("eks_row", [1, H], BF16)
        eqs_row = perm("eqs_row", [1, H], BF16)
        esk_row = perm("esk_row", [1, H], BF16)
        esq_row = perm("esq_row", [1, H], BF16)
        ep_tmp = perm("ep_tmp", [1, H], F32)
        si_all = perm("si_all", [128, NT, H], F32)
        sia_all = perm("sia_all", [128, NT, H], BF16)
        sc_tm = perm("sc_tm", [128, NT, H], BF16)
        kv2 = perm("kv2", [128, C, 128], BF16)
        mx16 = perm("mx16", [16, 1], F32)
        nmx16 = perm("nmx16", [16, 1], F32)
        se16 = perm("se16", [16, 1], F32)
        rse16 = perm("rse16", [16, 1], F32)
        nsum = perm("nsum", [128, 2], F32)
        ssq = perm("ssq", [128, 2], F32)
        e2c = perm("e2c", [128, 1], F32)
        nvc = perm("nvc", [128, 1], F32)
        vpe = perm("vpe", [128, 2], F32)
        rsig = perm("rsig", [128, 2], F32)
        nmo = o[0]
        negmu = perm("negmu", [128, 2], F32)
        sigc = perm("sigc", [128, 2], F32)
        nmsg = place("nmsg", [128, 10], F32, nmo)
        nmsgr = None
        sa_t = perm("sa_t", [128, 2, H], F32)
        so_t = perm("so_t", [128, 2, H], F32)
        assert o[0] <= 25 * KiB, o[0]

        h_fm = place("h_fm", [128, HC, 512], BF16, 0)
        wo_sb = place("wo_sb", [128, C, D], BF16, 25 * KiB)
        lnh2 = [place(f"lnh2_{i}", [128, C, 512], BF16, 25 * KiB + i * 8 * KiB)
                for i in range(2)]

        wq_sb = place("wq_sb", [128, C, D], BF16, 41 * KiB)
        wk_sb = place("wk_sb", [128, C, D], BF16, 57 * KiB)
        wv_sb = place("wv_sb", [128, C, D], BF16, 73 * KiB)
        wg_sb = place("wg_sb", [128, C, HID], BF16, 41 * KiB)

        b3 = 89 * KiB
        xt1 = [place(f"xt1_{i}", [128, D], F32, b3 + i * 4 * KiB) for i in range(3)]
        sq_scr = place("sq_scr", [128, D], F32, b3 + 12 * KiB)
        ln1t = [place(f"ln1t_{i}", [128, D], BF16, b3 + 16 * KiB + i * 2 * KiB)
                for i in range(3)]
        lnh1 = [place(f"lnh1_{i}", [128, C, 128], BF16, b3 + 22 * KiB + i * 2 * KiB)
                for i in range(4)]
        qt1 = [place(f"qt1_{i}", [128, D], BF16, b3 + 30 * KiB + i * 2 * KiB)
               for i in range(2)]
        kt1 = [place(f"kt1_{i}", [128, D], BF16, b3 + 34 * KiB + i * 2 * KiB)
               for i in range(2)]
        csq = place("csq", [1, D], BF16, b3 + 42 * KiB)
        csk = place("csk", [1, D], BF16, b3 + 44 * KiB)
        csv = place("csv", [1, D], BF16, b3 + 46 * KiB)
        vt1 = [place(f"vt1_{i}", [128, D], BF16, b3 + 38 * KiB + i * 2 * KiB)
               for i in range(2)]
        wu_sb = place("wu_sb", [128, C, HID], BF16, b3)

        b4 = 137 * KiB
        kfm = [place(f"kfm_{i}", [128, C, 512], BF16, b4 + i * 8 * KiB)
               for i in range(2)]
        csrc_fm = place("csrc_fm", [16, T], F32, b4 + 16 * KiB)
        exp_fm = place("exp_fm", [16, T], BF16, b4 + 32 * KiB)
        qxfm = [place(f"qxfm_{i}", [128, C, 512], BF16, b4 + i * 8 * KiB)
                for i in range(3)]
        x1t = [place(f"x1t_{i}", [128, D], F32, b4 + 24 * KiB + i * 4 * KiB)
               for i in range(2)]
        ln2t = [place(f"ln2t_{i}", [128, D], BF16, b4 + 32 * KiB + i * 2 * KiB)
                for i in range(2)]
        sq_scr2 = place("sq_scr2", [128, D], F32, b4 + 36 * KiB)
        wd_a = place("wd_a", [128, 20, D], BF16, b4)
        wd_b = place("wd_b", [128, 4, D], BF16, b4 + 40 * KiB)

        b5 = 177 * KiB
        qfm = [place(f"qfm_{i}", [128, C, 512], BF16, b5 + i * 8 * KiB)
               for i in range(2)]
        qst = [place(f"qst_{i}", [128, D], BF16, b5 + 16 * KiB + i * 2 * KiB)
               for i in range(2)]
        qsi_t = [place("qsi_0", [128, D], BF16, b5 + 20 * KiB)]
        kso_t = [place("kso_0", [128, D], BF16, b5 + 22 * KiB)]
        kst2 = [place(f"kst2_{i}", [128, D], BF16, b5 + 24 * KiB + i * 2 * KiB)
                for i in range(2)]
        kst = [place(f"kst_{i}", [128, D], BF16, b5 + i * 2 * KiB) for i in range(3)]
        vst = [place(f"vst_{i}", [128, D], BF16, b5 + 6 * KiB + i * 2 * KiB)
               for i in range(3)]
        vsc_t = [place(f"vsc_{i}", [128, D], BF16, b5 + 12 * KiB + i * 2 * KiB)
                 for i in range(2)]
        qst5 = [place(f"qst5_{i}", [128, D], BF16, b5 + 16 * KiB + i * 2 * KiB)
                for i in range(4)]
        qx_t = [place(f"qx_{i}", [128, D], BF16, b5 + 24 * KiB + i * 2 * KiB)
                for i in range(2)]
        xt5 = [place(f"xt5_{i}", [128, D], F32, b5 + i * 4 * KiB)
               for i in range(3)]
        sg_t = [place(f"sg_{i}", [128, 512], BF16, b5 + 8 * KiB + i * 1 * KiB)
                for i in range(2)]
        fout = [place(f"fout_{i}", [128, D], F32, b5 + 10 * KiB + i * 4 * KiB)
                for i in range(2)]
        c127 = place("c127", [128, 1], F32, b5 + 28 * KiB)
        nmsgr = [place(f"nmsgr_{i}", [1, 256], BF16, b5 + 28 * KiB + 32 + 512 * i)
                 for i in range(2)]
        assert b5 + 30 * KiB <= ARENA

        sems_list = []

        def sem(name2):
            h = ctx.enter_context(nc.semaphore(name2))
            sems_list.append(h)
            return h

        def sems(name, n):
            return [sem(f"{name}{i}") for i in range(n)]

        s_w = sem("s_w"); s_wo = sem("s_wo")
        s_x1 = sems("s_x1", 3); s_lnT = sems("s_lnT", 4)
        s_qw = sems("s_qw", 2); s_kw = sems("s_kw", 2); s_vw = sems("s_vw", 2)
        s_nmr = sems("s_nmr", 2)
        a_sq = sem("a_sq"); a_sig = sem("a_sig")
        d_vpe = sem("d_vpe"); a_sqr = sem("a_sqr"); d_ln1 = sem("d_ln1")
        d_vev = sem("d_vev"); t_q = sem("t_q"); t_k = sem("t_k"); t_v = sem("t_v")
        t_sum = sem("t_sum"); d_rows = sem("d_rows"); t_cols = sem("t_cols")
        d_blk = sem("d_blk"); s_q2 = sems("s_q2", 2); s_k2 = sems("s_k2", 2)
        s_fm2 = sems("s_fm2", 2)
        t_dA = sem("t_dA"); t_dB = sem("t_dB"); d_rec = sem("d_rec")
        d_p2 = sem("d_p2"); t_sum2 = sem("t_sum2"); s_fm3 = sems("s_fm3", 2)
        t_C = sem("t_C"); d_C = sem("d_C"); t_D = sem("t_D"); a_sa = sem("a_sa")
        d_sia = sem("d_sia"); d_mx = sem("d_mx"); a_exp = sem("a_exp")
        d_scf = sem("d_scf"); t_scT = sem("t_scT"); d_sct = sem("d_sct")
        s_k4 = sems("s_k4", 3); s_v4 = sems("s_v4", 3)
        d_vsc = sem("d_vsc"); t_kv4 = sem("t_kv4")
        d_kv2 = sem("d_kv2"); s_q5 = sems("s_q5", 4); s_x5 = sems("s_x5", 3)
        d_qx = sem("d_qx"); t_ko = sem("t_ko")
        t_qxT = sem("t_qxT"); a_qx = sem("a_qx")
        e5a = sem("e5a"); e5d = sem("e5d"); t_wo = sem("t_wo"); d_x1 = sem("d_x1")
        a_sq2 = sem("a_sq2"); d_vp2 = sem("d_vp2"); a_sr2 = sem("a_sr2")
        d_ln2 = sem("d_ln2"); s_out1 = sems("s_out1", 2); s_out2 = sems("s_out2", 2)
        s_ln2w = sems("s_ln2w", 2)
        s_wf = sem("s_wf"); s_wd = sem("s_wd"); s_lh2 = sems("s_lh2", 2)
        t_G = sem("t_G"); t_U = sem("t_U")
        s_cst = sem("s_cst")
        a_sg = sem("a_sg"); d_h = sem("d_h"); t_dn = sem("t_dn"); d_fo = sem("d_fo")
        s_dbg = sem("s_dbg"); d_self = sem("d_self")
        dsc = [0]

        P = [ctx.enter_context(nc.psum_tensor(f"P{i}", [128, 512], F32))
             for i in range(8)]
        P0, P1, P2, P3, P4, P5, P6, P7 = P

        def bfv(ps, cols):
            return ps[:, :].bitcast(BF16)[:, 0:cols]

        block = ctx.enter_context(nc.Block())

        all_sems = [h for h in sems_list]
        nc.all_engine_barrier()
        lo = min(h.num for h in all_sems)
        hi = max(h.num for h in all_sems)
        nc.gpsimd.dma_reset(range(lo, hi + 1))
        nc.gpsimd.sem_clear(range(lo, hi + 1))
        nc.all_engine_barrier()

        @block.gpsimd
        def _(g):
            g.memset(ones_r[:], 1.0).then_inc(s_cst, 1)
            g.memset(ones_c[:], 1.0).then_inc(s_cst, 1)
            g.memset(c127[:], QSC).then_inc(s_cst, 1)
            g.memset(ksb[:], 0.0).then_inc(s_cst, 1)
            g.memset(qsb[:], 0.0).then_inc(s_cst, 1)
            g.memset(skb[:], 0.0).then_inc(s_cst, 1)
            g.memset(sqb16[:], 0.0).then_inc(s_cst, 1)
            g.memset(kv2[:], 0.0).then_inc(s_cst, 1)
            g.memset(identb[:], 0.0).then_inc(s_cst, 1)
            g.wait_ge(s_cst, 9)
            g.affine_select(out=identb[:], in_=identb[:],
                            compare_op=ALU.not_equal, fill=1.0, base=0,
                            pattern=[[-1, 128]], channel_multiplier=1
                            ).then_inc(s_cst, 1)
            g.dma_start(out=wq_sb[:], in_=wq_d.rearrange("(c p) i -> p c i", p=128)
                        ).then_inc(s_w, 16)
            g.dma_start(out=wk_sb[:], in_=wk_d.rearrange("(c p) i -> p c i", p=128)
                        ).then_inc(s_w, 16)
            g.dma_start(out=wv_sb[:], in_=wv_d.rearrange("(c p) i -> p c i", p=128)
                        ).then_inc(s_w, 16)
            g.dma_start(out=b_sb[:], in_=b_d).then_inc(s_w, 16)
            g.dma_start(out=wo_sb[:], in_=wo_d.rearrange("(c p) i -> p c i", p=128)
                        ).then_inc(s_wo, 16)
            for t in range(NT):
                if t >= 3:
                    g.wait_ge(a_sq, 2 * (t - 2))
                    g.wait_ge(d_ln1, t - 2)
                g.dma_start(out=xt1[t % 3][:], in_=x_d[t * 128:(t + 1) * 128, :]
                            ).then_inc(s_x1[t % 3], 16)
                if t >= 1:
                    tt = t - 1
                    g.wait_ge(a_sig, 2 * tt + 1)
                    g.dma_start(out=q_sc[tt * 128:(tt + 1) * 128, :],
                                in_=qt1[tt % 2][:]).then_inc(s_qw[tt % 2], 16)
                    g.wait_ge(a_sig, 2 * tt + 2)
                    g.dma_start(out=k_sc[tt * 128:(tt + 1) * 128, :],
                                in_=kt1[tt % 2][:]).then_inc(s_kw[tt % 2], 16)
                    g.wait_ge(d_vev, tt + 1)
                    g.dma_start(out=v_sc[tt * 128:(tt + 1) * 128, :],
                                in_=vt1[tt % 2][:]).then_inc(s_vw[tt % 2], 16)
                if t >= 2:
                    g.wait_ge(t_v, t - 1)
                g.wait_ge(d_vpe, t + 1)
                g.wait_ge(a_sqr, t + 1)
                g.dma_start(out=nmsgr[t % 2][0:1, :].rearrange(
                                "a (p j) -> a p j", j=2),
                            in_=nmsg[:, t % 2::8],
                            allow_slow_non_contiguous=True
                            ).then_inc(s_nmr[t % 2], 16)
            tt = NT - 1
            g.wait_ge(a_sig, 2 * tt + 1)
            g.dma_start(out=q_sc[tt * 128:, :], in_=qt1[tt % 2][:]).then_inc(s_qw[tt % 2], 16)
            g.wait_ge(a_sig, 2 * tt + 2)
            g.dma_start(out=k_sc[tt * 128:, :], in_=kt1[tt % 2][:]).then_inc(s_kw[tt % 2], 16)
            g.wait_ge(d_vev, tt + 1)
            g.dma_start(out=v_sc[tt * 128:, :], in_=vt1[tt % 2][:]).then_inc(s_vw[tt % 2], 16)
            if upto < 2:
                for s_ in (s_qw + s_kw + s_vw):
                    g.wait_ge(s_, 16 * (NT // 2))
                if debug:
                    g.dma_start(out=dbg_ln1, in_=ln1t[2][:]).then_inc(s_dbg, 16)
                    g.dma_start(out=dbg_lnh, in_=lnh1[2][:]).then_inc(s_dbg, 16)
                    g.dma_start(out=dbg_st[:, 0:2], in_=ssq[:], allow_slow_non_contiguous=True).then_inc(s_dbg, 16)
                    g.dma_start(out=dbg_st[:, 2:3], in_=negmu[:], allow_slow_non_contiguous=True).then_inc(s_dbg, 16)
                    g.dma_start(out=dbg_st[:, 3:4], in_=rsig[:], allow_slow_non_contiguous=True).then_inc(s_dbg, 16)
                    g.dma_start(out=dbg_st[:, 4:6], in_=vpe[:], allow_slow_non_contiguous=True).then_inc(s_dbg, 16)
                    g.dma_start(out=dbg_st[:, 6:8], in_=sigc[:], allow_slow_non_contiguous=True).then_inc(s_dbg, 16)
                    g.wait_ge(s_dbg, 16 * 7)
                return
            for t in range(NT):
                if t >= 2:
                    g.wait_ge(d_p2, t - 1)
                g.dma_start(out=qst[t % 2][:], in_=q_sc[t * 128:(t + 1) * 128, :]
                            ).then_inc(s_q2[t % 2], 16)
                g.dma_start(out=kst2[t % 2][:],
                            in_=k_sc[t * 128:(t + 1) * 128, :]).then_inc(s_k2[t % 2], 16)
            if upto < 3:
                return
            g.dma_start(out=wg_sb[:], in_=wg_d.rearrange("(c p) h -> p c h", p=128)
                        ).then_inc(s_wf, 16)
            g.dma_start(out=wu_sb[:], in_=wu_d.rearrange("(c p) h -> p c h", p=128)
                        ).then_inc(s_wf, 16)
            if upto < 4:
                return
            if upto < 5:
                return
            for tt in range(NT):
                g.wait_ge(d_x1, tt + 1)
                g.dma_start(out=out_d[tt * 128:(tt + 1) * 128, :],
                            in_=x1t[tt % 2][:]).then_inc(s_out1[tt % 2], 16)
                g.wait_ge(d_ln2, tt + 1)
                g.dma_start(out=ln2_sc[tt * 128:(tt + 1) * 128, :],
                            in_=ln2t[tt % 2][:]).then_inc(s_ln2w[tt % 2], 16)
            if upto < 6:
                for s_ in s_out1 + s_ln2w:
                    g.wait_ge(s_, 16 * (NT // 2))
                if debug:
                    g.wait_ge(d_kv2, 1)
                    g.dma_start(out=dbg_si, in_=si_all[:]).then_inc(s_dbg, 16)
                    g.dma_start(out=dbg_sia, in_=sia_all[:]).then_inc(s_dbg, 16)
                    g.dma_start(out=dbg_sc, in_=sc_tm[:]).then_inc(s_dbg, 16)
                    g.dma_start(out=dbg_kv, in_=kv2[:]).then_inc(s_dbg, 16)
                    g.wait_ge(s_dbg, 16 * 4)
                return
            g.wait_ge(s_out1[0], 16 * (NT // 2))
            g.wait_ge(s_out1[1], 16 * (NT // 2))
            g.wait_ge(s_ln2w[0], 16 * (NT // 2))
            g.wait_ge(s_ln2w[1], 16 * (NT // 2))
            g.dma_start(out=wd_a[:],
                        in_=wd_d[0:20 * 128, :].rearrange("(c p) d -> p c d", p=128)
                        ).then_inc(s_wd, 16)
            g.dma_start(out=wd_b[:],
                        in_=wd_d[20 * 128:, :].rearrange("(c p) d -> p c d", p=128)
                        ).then_inc(s_wd, 16)
            if debug:
                g.wait_ge(d_kv2, 1)
                g.dma_start(out=dbg_si, in_=si_all[:]).then_inc(s_dbg, 16)
                g.dma_start(out=dbg_sia, in_=sia_all[:]).then_inc(s_dbg, 16)
                g.dma_start(out=dbg_sc, in_=sc_tm[:]).then_inc(s_dbg, 16)
                g.dma_start(out=dbg_kv, in_=kv2[:]).then_inc(s_dbg, 16)
                g.dma_start(out=dbg_rows[0:1, 0:D], in_=ks_row[:]).then_inc(s_dbg, 16)
                g.dma_start(out=dbg_rows[0:1, D:2 * D], in_=qs_row[:]).then_inc(s_dbg, 16)
                g.dma_start(out=dbg_rows[0:1, 2 * D:3 * D], in_=sk_row[:]).then_inc(s_dbg, 16)
                g.dma_start(out=dbg_rows[0:1, 3 * D:4 * D], in_=sq_row[:]).then_inc(s_dbg, 16)
            for gi in range(NG):
                for t4 in range(4):
                    t = gi * 4 + t4
                    g.wait_ge(d_fo, 4 * gi + t4 + 1)
                    g.dma_start(out=out_d[t * 128:(t + 1) * 128, :],
                                in_=fout[t4 % 2][:], accum_op=ALU.add
                                ).then_inc(s_out2[t4 % 2], 16)
            g.wait_ge(s_out2[0], 16 * (NT // 2))
            g.wait_ge(s_out2[1], 16 * (NT // 2))
            if debug:
                g.wait_ge(s_dbg, 16 * 8)

        @block.sync
        def _(sy):
            for t in range(NT):
                sy.wait_ge(d_ln1, t + 1)
                if t >= 4:
                    sy.wait_ge(t_v, t - 3)
                for c in range(C):
                    sy.dma_start_transpose(lnh1[t % 4][:, c, :],
                                           ln1t[t % 3][:, c * 128:(c + 1) * 128]
                                           ).then_inc(s_lnT[t % 4], 16)
            if upto < 2:
                return
            sy.wait_ge(s_qw[0], 16 * (NT // 2))
            sy.wait_ge(s_qw[1], 16 * (NT // 2))
            sy.wait_ge(s_kw[0], 16 * (NT // 2))
            sy.wait_ge(s_kw[1], 16 * (NT // 2))
            for gi in range(NG):
                if gi >= 2:
                    sy.wait_ge(t_dB, (gi - 1) * 4)
                for c in range(C):
                    sy.dma_start_transpose(
                        qfm[gi % 2][:, c, :],
                        q_sc[gi * 512:(gi + 1) * 512, c * 128:(c + 1) * 128]
                    ).then_inc(s_fm2[gi % 2], 16)
                for c in range(C):
                    sy.dma_start_transpose(
                        kfm[gi % 2][:, c, :],
                        k_sc[gi * 512:(gi + 1) * 512, c * 128:(c + 1) * 128]
                    ).then_inc(s_fm2[gi % 2], 16)
            if upto < 3:
                return
            for gi in range(NG):
                if gi == 0:
                    sy.wait_ge(t_dB, 4 * (NG - 1))
                elif gi == 1:
                    sy.wait_ge(t_sum2, NT)
                else:
                    sy.wait_ge(t_D, (gi - 1) * 4)
                    sy.wait_ge(t_C, gi - 1)
                for c in range(C):
                    sy.dma_start_transpose(
                        qfm[gi % 2][:, c, :],
                        q_sc[gi * 512:(gi + 1) * 512, c * 128:(c + 1) * 128]
                    ).then_inc(s_fm3[gi % 2], 16)
                for c in range(C):
                    sy.dma_start_transpose(
                        kfm[gi % 2][:, c, :],
                        k_sc[gi * 512:(gi + 1) * 512, c * 128:(c + 1) * 128]
                    ).then_inc(s_fm3[gi % 2], 16)
            if upto < 4:
                return
            sy.wait_ge(t_D, NT)
            sy.wait_ge(t_scT, NT)
            sy.wait_ge(s_vw[0], 16 * (NT // 2))
            sy.wait_ge(s_vw[1], 16 * (NT // 2))
            sy.wait_ge(d_p2, NT)
            for t in range(NT):
                if t >= 3:
                    sy.wait_ge(t_kv4, t - 2)
                    sy.wait_ge(d_vsc, t - 2)
                sy.dma_start(out=kst[t % 3][:], in_=k_sc[t * 128:(t + 1) * 128, :]
                             ).then_inc(s_k4[t % 3], 16)
                sy.dma_start(out=vst[t % 3][:], in_=v_sc[t * 128:(t + 1) * 128, :]
                             ).then_inc(s_v4[t % 3], 16)
                if t < 4:
                    sy.dma_start(out=qst5[t % 4][:],
                                 in_=q_sc[t * 128:(t + 1) * 128, :]
                                 ).then_inc(s_q5[t % 4], 16)
            if upto < 5:
                return
            for t in range(4, NT):
                sy.wait_ge(d_qx, t - 3)
                sy.dma_start(out=qst5[t % 4][:],
                             in_=q_sc[t * 128:(t + 1) * 128, :]
                             ).then_inc(s_q5[t % 4], 16)
            if upto < 6:
                return
            for gi in range(NG):
                if gi == 0:
                    sy.wait_ge(t_wo, NT)
                sy.wait_ge(s_ln2w[0], 16 * (2 * gi + 2))
                sy.wait_ge(s_ln2w[1], 16 * (2 * gi + 2))
                if gi >= 2:
                    sy.wait_ge(t_U, HC * (gi - 1))
                for c in range(C):
                    sy.dma_start_transpose(
                        lnh2[gi % 2][:, c, :],
                        ln2_sc[gi * 512:(gi + 1) * 512, c * 128:(c + 1) * 128]
                    ).then_inc(s_lh2[gi % 2], 16)

        @block.tensor
        def _(t_):
            t_.wait_ge(s_cst, 10)
            t_.wait_ge(s_w, 16 * 4)
            for c in range(C):
                lc = (c == C - 1)
                t_.matmul(P0[0:1, :], ones_c[:], wq_sb[:, c, 0:512],
                          start=(c == 0), stop=lc)
                t_.matmul(P1[0:1, :], ones_c[:], wq_sb[:, c, 512:1024],
                          start=(c == 0), stop=lc)
                t_.matmul(P2[0:1, :], ones_c[:], wk_sb[:, c, 0:512],
                          start=(c == 0), stop=lc)
                t_.matmul(P3[0:1, :], ones_c[:], wk_sb[:, c, 512:1024],
                          start=(c == 0), stop=lc)
                t_.matmul(P4[0:1, :], ones_c[:], wv_sb[:, c, 0:512],
                          start=(c == 0), stop=lc)
                mcs = t_.matmul(P5[0:1, :], ones_c[:], wv_sb[:, c, 512:1024],
                                start=(c == 0), stop=lc)
            mcs.then_inc(t_cols, 1)
            t_.wait_ge(d_rows, 6)
            for t in range(NT):
                t_.wait_ge(s_lnT[t % 4], 16 * 8 * (t // 4 + 1))
                lnh = lnh1[t % 4]
                if t >= 1:
                    t_.wait_ge(d_vev, t)
                for c in range(C):
                    t_.matmul(P0[:], lnh[:, c, :], wq_sb[:, c, 0:512],
                              start=(c == 0), stop=False)
                    t_.matmul(P1[:], lnh[:, c, :], wq_sb[:, c, 512:1024],
                              start=(c == 0), stop=False)
                t_.wait_ge(s_nmr[t % 2], 16 * (t // 2 + 1))
                nmr = nmsgr[t % 2][0:1, 0::2]
                sgr = nmsgr[t % 2][0:1, 1::2]
                t_.matmul(P0[:], nmr, csq[0:1, 0:512], start=False, stop=False)
                t_.matmul(P1[:], nmr, csq[0:1, 512:1024], start=False, stop=False)
                t_.matmul(P0[:], sgr, b_sb[0:1, 0:512],
                          start=False, stop=True)
                t_.matmul(P1[:], sgr, b_sb[0:1, 512:1024],
                          start=False, stop=True).then_inc(t_q, 1)
                if t >= 1:
                    t_.wait_ge(a_sig, 2 * t)
                for c in range(C):
                    t_.matmul(P2[:], lnh[:, c, :], wk_sb[:, c, 0:512],
                              start=(c == 0), stop=False)
                    t_.matmul(P3[:], lnh[:, c, :], wk_sb[:, c, 512:1024],
                              start=(c == 0), stop=False)
                t_.matmul(P2[:], nmr, csk[0:1, 0:512], start=False, stop=False)
                t_.matmul(P3[:], nmr, csk[0:1, 512:1024], start=False, stop=False)
                t_.matmul(P2[:], sgr, b_sb[0:1, 1024:1536],
                          start=False, stop=True)
                t_.matmul(P3[:], sgr, b_sb[0:1, 1536:2048],
                          start=False, stop=True).then_inc(t_k, 1)
                t_.wait_ge(a_sig, 2 * t + 1)
                for c in range(C):
                    t_.matmul(P0[:], lnh[:, c, :], wv_sb[:, c, 0:512],
                              start=(c == 0), stop=False)
                    t_.matmul(P1[:], lnh[:, c, :], wv_sb[:, c, 512:1024],
                              start=(c == 0), stop=False)
                t_.matmul(P0[:], nmr, csv[0:1, 0:512], start=False, stop=False)
                t_.matmul(P1[:], nmr, csv[0:1, 512:1024], start=False, stop=False)
                t_.matmul(P0[:], sgr, b_sb[0:1, 2048:2560],
                          start=False, stop=True)
                t_.matmul(P1[:], sgr, b_sb[0:1, 2560:3072],
                          start=False, stop=True).then_inc(t_v, 1)
                t_.wait_ge(a_sig, 2 * t + 2)
                t_.matmul(P4[0:1, :], ones_c[:], qt1[t % 2][:, 0:512],
                          start=(t == 0), stop=(t == NT - 1))
                t_.matmul(P5[0:1, :], ones_c[:], qt1[t % 2][:, 512:1024],
                          start=(t == 0), stop=(t == NT - 1))
                t_.matmul(P6[0:1, :], ones_c[:], kt1[t % 2][:, 0:512],
                          start=(t == 0), stop=(t == NT - 1))
                t_.matmul(P7[0:1, :], ones_c[:], kt1[t % 2][:, 512:1024],
                          start=(t == 0), stop=(t == NT - 1)).then_inc(t_sum, 1)
            if upto < 2:
                return
            pscol = bfv(P0, 32)
            t_.wait_ge(d_rows, 7)
            for c in range(C):
                t_.transpose(pscol[:, 2 * c:2 * c + 1],
                             ks_row[0:1, c * 128:(c + 1) * 128], identb[0:1, 0:1])
            for c in range(C):
                mmt = t_.transpose(pscol[:, 16 + 2 * c:16 + 2 * c + 1],
                                   qs_row[0:1, c * 128:(c + 1) * 128],
                                   identb[0:1, 0:1])
            mmt.then_inc(t_cols, 1)
            t_.wait_ge(d_blk, 1)
            psA16 = [P0[:, 0:16], P1[:, 0:16]]
            psB16 = [P2[:, 0:16], P3[:, 0:16]]
            for t in range(NT):
                gi = t // 4
                t_.wait_ge(s_fm2[gi % 2], 16 * 16 * (gi // 2 + 1))
                if t >= 2:
                    t_.wait_ge(d_rec, 2 * (t - 1))
                pa = psA16[t % 2]
                for c in range(C):
                    t_.matmul(pa[:, 2 * c:2 * c + 2],
                              qfm[gi % 2][:, c, (t % 4) * 128:(t % 4 + 1) * 128],
                              ksb[:, c, :], start=(c == 0), stop=False,
                              skip_group_check=True)
                t_.matmul(pa, ones_r[0:1, 0:128], eks_row[:],
                          start=False, stop=True, skip_group_check=True
                          ).then_inc(t_dA, 1)
                pb = psB16[t % 2]
                for c in range(C):
                    t_.matmul(pb[:, 2 * c:2 * c + 2],
                              kfm[gi % 2][:, c, (t % 4) * 128:(t % 4 + 1) * 128],
                              qsb[:, c, :], start=(c == 0), stop=False,
                              skip_group_check=True)
                t_.matmul(pb, ones_r[0:1, 0:128], eqs_row[:],
                          start=False, stop=True, skip_group_check=True
                          ).then_inc(t_dB, 1)
                t_.wait_ge(d_p2, t + 1)
                t_.matmul(P4[0:1, :], ones_c[:], qsi_t[0][:, 0:512],
                          start=(t == 0), stop=(t == NT - 1))
                t_.matmul(P5[0:1, :], ones_c[:], qsi_t[0][:, 512:1024],
                          start=(t == 0), stop=(t == NT - 1))
                t_.matmul(P6[0:1, :], ones_c[:], kso_t[0][:, 0:512],
                          start=(t == 0), stop=(t == NT - 1))
                t_.matmul(P7[0:1, :], ones_c[:], kso_t[0][:, 512:1024],
                          start=(t == 0), stop=(t == NT - 1)).then_inc(t_sum2, 1)
            t_.wait_ge(d_rows, 8)
            for c in range(C):
                t_.transpose(pscol[:, 2 * c:2 * c + 1],
                             sk_row[0:1, c * 128:(c + 1) * 128], identb[0:1, 0:1])
            for c in range(C):
                mmt = t_.transpose(pscol[:, 16 + 2 * c:16 + 2 * c + 1],
                                   sq_row[0:1, c * 128:(c + 1) * 128],
                                   identb[0:1, 0:1])
            mmt.then_inc(t_cols, 1)
            if upto < 3:
                return
            t_.wait_ge(d_blk, 2)
            t_.wait_ge(d_rec, 2 * NT)
            for gi in range(NG):
                t_.wait_ge(s_fm3[gi % 2], 16 * 16 * (gi // 2 + 1))
                if gi >= 2:
                    t_.wait_ge(d_C, gi - 1)
                pc = [P2[0:16, :], P3[0:16, :]][gi % 2]
                for c in range(C):
                    t_.matmul(pc, sqb16[:, c, :], kfm[gi % 2][:, c, :],
                              start=(c == 0), stop=False, skip_group_check=True)
                t_.matmul(pc, esq_row[:], ones_r[:], start=False, stop=True,
                          skip_group_check=True).then_inc(t_C, 1)
                for t4 in range(4):
                    t = gi * 4 + t4
                    if t >= 2:
                        t_.wait_ge(a_sa, t - 1)
                    pd = psA16[t % 2]
                    for c in range(C):
                        t_.matmul(pd[:, 2 * c:2 * c + 2],
                                  qfm[gi % 2][:, c, t4 * 128:(t4 + 1) * 128],
                                  skb[:, c, :], start=(c == 0), stop=False,
                                  skip_group_check=True)
                    t_.matmul(pd, ones_r[0:1, 0:128], esk_row[:],
                              start=False, stop=True, skip_group_check=True
                              ).then_inc(t_D, 1)
            t_.wait_ge(d_scf, 1)
            psT = [bfv(P4, 16), bfv(P5, 16)]
            for i in range(NT):
                if i >= 2:
                    t_.wait_ge(d_sct, i - 1)
                t_.transpose(psT[i % 2][:, :],
                             exp_fm[0:16, i * 128:(i + 1) * 128],
                             identb[0:16, 0:16]).then_inc(t_scT, 1)
            if upto < 4:
                return
            kvps = [P2, P3]
            for t in range(NT):
                t_.wait_ge(s_k4[t % 3], 16 * (t // 3 + 1))
                t_.wait_ge(d_vsc, t + 1)
                mm = None
                for c in range(C):
                    mm = t_.matmul(
                        kvps[c // 4][:, (c % 4) * 128:(c % 4 + 1) * 128],
                        kst[t % 3][:, c * 128:(c + 1) * 128],
                        vsc_t[t % 2][:, c * 128:(c + 1) * 128],
                        start=(t == 0 and c % 4 == 0),
                        stop=(t == NT - 1 and c % 4 == 3),
                        skip_group_check=True)
                mm.then_inc(t_kv4, 1)
            if upto < 5:
                return
            t_.wait_ge(d_kv2, 1)
            t_.wait_ge(s_wo, 16)
            psO = [P0, P1]
            pqx = [bfv(P2, 1024), bfv(P3, 1024)]
            for gi in range(NG):
                for t4 in range(4):
                    t = gi * 4 + t4
                    t_.wait_ge(d_qx, t + 1)
                    if t >= 2:
                        t_.wait_ge(a_qx, t - 1)
                    mmq = None
                    for c in range(C):
                        mmq = t_.transpose(pqx[t % 2][:, c * 128:(c + 1) * 128],
                                           qx_t[t % 2][:, c * 128:(c + 1) * 128],
                                           identb[:, :])
                    mmq.then_inc(t_qxT, 1)
                t_.wait_ge(a_qx, 4 * (gi + 1))
                for c in range(C):
                    idx = gi * C + c
                    if idx >= 2:
                        prev = idx - 2
                        if prev % 2 == 0:
                            t_.wait_ge(e5a, prev // 2 + 1)
                        else:
                            t_.wait_ge(e5d, prev // 2 + 1)
                    t_.matmul(psO[c % 2][:], kv2[:, c, :], qxfm[gi % 3][:, c, :],
                              start=True, stop=True).then_inc(t_ko, 1)
                t_.wait_ge(e5a, 4 * (gi + 1))
                t_.wait_ge(e5d, 4 * (gi + 1))
                for t4 in range(4):
                    t = gi * 4 + t4
                    if t >= 2:
                        t_.wait_ge(d_x1, t - 1)
                    pw0, pw1 = [(P4, P5), (P6, P7)][t % 2]
                    for c in range(C):
                        t_.matmul(pw0[:], qxfm[gi % 3][:, c, t4 * 128:(t4 + 1) * 128],
                                  wo_sb[:, c, 0:512], start=(c == 0), stop=False)
                        t_.matmul(pw1[:], qxfm[gi % 3][:, c, t4 * 128:(t4 + 1) * 128],
                                  wo_sb[:, c, 512:1024], start=(c == 0), stop=False)
                    t_.matmul(pw0[:], ones_r[0:1, 0:128], b_sb[0:1, 3072:3584],
                              start=False, stop=True)
                    t_.matmul(pw1[:], ones_r[0:1, 0:128], b_sb[0:1, 3584:4096],
                              start=False, stop=True).then_inc(t_wo, 1)
            if upto < 6:
                return
            t_.wait_ge(d_x1, NT)
            t_.wait_ge(s_wf, 32)
            psG = [P0, P1]
            psU = [P2, P3]
            for gi in range(NG):
                t_.wait_ge(s_lh2[gi % 2], 16 * 8 * (gi // 2 + 1))
                for hc in range(HC):
                    idx = gi * HC + hc
                    if idx >= 2:
                        t_.wait_ge(a_sg, idx - 1)
                    mmg = None
                    for j in range(C):
                        mmg = t_.matmul(psG[hc % 2][:],
                                        wg_sb[:, j, hc * 128:(hc + 1) * 128],
                                        lnh2[gi % 2][:, j, :],
                                        start=(j == 0), stop=(j == C - 1))
                    mmg.then_inc(t_G, 1)
                    if idx >= 2:
                        t_.wait_ge(d_h, idx - 1)
                    mmu = None
                    for j in range(C):
                        mmu = t_.matmul(psU[hc % 2][:],
                                        wu_sb[:, j, hc * 128:(hc + 1) * 128],
                                        lnh2[gi % 2][:, j, :],
                                        start=(j == 0), stop=(j == C - 1))
                    mmu.then_inc(t_U, 1)
                t_.wait_ge(d_h, HC * (gi + 1))
                if gi == 0:
                    t_.wait_ge(s_wd, 32)
                for t4 in range(4):
                    gt = gi * 4 + t4
                    if gt >= 2:
                        t_.wait_ge(d_fo, gt - 1)
                    pd0, pd1 = [(P4, P5), (P6, P7)][t4 % 2]
                    def wd_at(hc):
                        return wd_a[:, hc, :] if hc < 20 else wd_b[:, hc - 20, :]
                    for hc in range(HC):
                        t_.matmul(pd0[:], h_fm[:, hc, t4 * 128:(t4 + 1) * 128],
                                  wd_at(hc)[:, 0:512], start=(hc == 0),
                                  stop=(hc == HC - 1))
                    mmd = None
                    for hc in range(HC):
                        mmd = t_.matmul(pd1[:], h_fm[:, hc, t4 * 128:(t4 + 1) * 128],
                                        wd_at(hc)[:, 512:1024], start=(hc == 0),
                                        stop=(hc == HC - 1))
                    mmd.then_inc(t_dn, 1)

        @block.scalar
        def _(s):
            s.wait_ge(s_cst, 10)
            rcm = lambda t: 3 * t + 3
            s.wait_ge(t_cols, 1)
            s.activation(csq[0:1, 0:512], P0[0:1, :], AF.Copy)
            s.activation(csq[0:1, 512:1024], P1[0:1, :], AF.Copy)
            s.activation(csk[0:1, 0:512], P2[0:1, :], AF.Copy)
            s.activation(csk[0:1, 512:1024], P3[0:1, :], AF.Copy)
            s.activation(csv[0:1, 0:512], P4[0:1, :], AF.Copy)
            s.activation(csv[0:1, 512:1024], P5[0:1, :], AF.Copy
                         ).then_inc(d_rows, 6)
            for t in range(NT):
                s.wait_ge(s_x1[t % 3], 16 * (t // 3 + 1))
                if t >= 2:
                    s.wait_ge(d_vpe, t - 1)
                s.activation(sq_scr[:], xt1[t % 3][:], AF.Square,
                             accum_out=ssq[:, t % 2:t % 2 + 1]).then_inc(a_sq, 1)
                s.activation(sq_scr[:], xt1[t % 3][:], AF.Copy,
                             accum_out=nsum[:, t % 2:t % 2 + 1]).then_inc(a_sq, 1)
                s.wait_ge(d_vpe, t + 1)
                if t >= 2:
                    s.wait_ge(d_self, rcm(t - 2))
                s.activation(sigc[:, t % 2:t % 2 + 1], vpe[:, t % 2:t % 2 + 1],
                             AF.Sqrt).then_inc(a_sqr, 1)
                s.wait_ge(t_q, t + 1)
                if t >= 2:
                    s.wait_ge(t_sum, t - 1)
                    s.wait_ge(s_qw[t % 2], 16 * (t // 2))
                s.wait_ge(d_self, rcm(t))
                s.activation(qt1[t % 2][:, 0:512], P0[:], AF.Sigmoid,
                             scale=rsig[:, t % 2:t % 2 + 1])
                s.activation(qt1[t % 2][:, 512:1024], P1[:], AF.Sigmoid,
                             scale=rsig[:, t % 2:t % 2 + 1]).then_inc(a_sig, 1)
                s.wait_ge(t_k, t + 1)
                if t >= 2:
                    s.wait_ge(s_kw[t % 2], 16 * (t // 2))
                s.activation(kt1[t % 2][:, 0:512], P2[:], AF.Sigmoid,
                             scale=rsig[:, t % 2:t % 2 + 1])
                s.activation(kt1[t % 2][:, 512:1024], P3[:], AF.Sigmoid,
                             scale=rsig[:, t % 2:t % 2 + 1]).then_inc(a_sig, 1)
            if upto < 3:
                return
            psA16 = [P0[:, 0:16], P1[:, 0:16]]
            for t in range(NT):
                s.wait_ge(t_D, t + 1)
                if t >= 2:
                    s.wait_ge(d_sia, t - 1)
                s.activation(sa_t[:, t % 2, :], psA16[t % 2], AF.Sigmoid
                             ).then_inc(a_sa, 1)
            s.wait_ge(d_mx, 1)
            s.activation(exp_fm[:], csrc_fm[:], AF.Exp, bias=nmx16[:],
                         accum_out=se16[:]).then_inc(a_exp, 1)
            if upto < 5:
                return
            s.wait_ge(d_kv2, 1)
            for t in range(3):
                s.dma_start(out=xt5[t % 3][:], in_=x_d[t * 128:(t + 1) * 128, :]
                            ).then_inc(s_x5[t % 3], 16)
            pqx = [bfv(P2, 1024), bfv(P3, 1024)]
            for gi in range(NG):
                for t4 in range(4):
                    t = gi * 4 + t4
                    s.wait_ge(t_qxT, t + 1)
                    s.activation(qxfm[gi % 3][:, 0:4, t4 * 128:(t4 + 1) * 128],
                                 pqx[t % 2][:, 0:512].rearrange(
                                     "p (c x) -> p c x", c=4), AF.Copy)
                    s.activation(qxfm[gi % 3][:, 4:8, t4 * 128:(t4 + 1) * 128],
                                 pqx[t % 2][:, 512:1024].rearrange(
                                     "p (c x) -> p c x", c=4), AF.Copy
                                 ).then_inc(a_qx, 1)
                for c in range(0, C, 2):
                    s.wait_ge(t_ko, gi * C + c + 1)
                    s.activation(qxfm[gi % 3][:, c, :], [P0, P1][c % 2][:],
                                 AF.Copy).then_inc(e5a, 1)
                for t4 in range(4):
                    t = gi * 4 + t4
                    s.wait_ge(d_x1, t + 1)
                    if t >= 2:
                        s.wait_ge(d_vp2, t - 1)
                    s.activation(sq_scr2[:], x1t[t % 2][:], AF.Square,
                                 accum_out=ssq[:, t % 2:t % 2 + 1]).then_inc(a_sq2, 1)
                    s.activation(sq_scr2[:], x1t[t % 2][:], AF.Copy,
                                 accum_out=nsum[:, t % 2:t % 2 + 1]).then_inc(a_sq2, 1)
                    if t + 3 < NT:
                        s.dma_start(out=xt5[(t + 3) % 3][:],
                                    in_=x_d[(t + 3) * 128:(t + 4) * 128, :]
                                    ).then_inc(s_x5[t % 3], 16)
                    s.wait_ge(d_vp2, t + 1)
                    if t >= 2:
                        s.wait_ge(d_ln2, t - 1)
                    s.activation(sigc[:, t % 2:t % 2 + 1], vpe[:, t % 2:t % 2 + 1],
                                 AF.Sqrt).then_inc(a_sr2, 1)
            if upto < 6:
                return
            for gi in range(NG):
                for hc in range(HC):
                    idx = gi * HC + hc
                    s.wait_ge(t_G, idx + 1)
                    if idx >= 2:
                        s.wait_ge(d_h, idx - 1)
                    s.activation(sg_t[hc % 2][:], [P0, P1][hc % 2][:],
                                 AF.Silu, scale=c127[:]).then_inc(a_sg, 1)

        @block.vector
        def _(v):
            v.wait_ge(s_cst, 10)
            for t in range(NT + 1):
                if t < NT:
                    v.wait_ge(s_x1[t % 3], 16 * (t // 3 + 1))
                    if t >= 3:
                        v.wait_ge(s_lnT[(t - 3) % 4], 16 * 8 * ((t - 3) // 4 + 1))
                    v.tensor_copy(ln1t[t % 3][:], xt1[t % 3][:]).then_inc(d_ln1, 1)
                if t >= 1:
                    tt = t - 1
                    v.wait_ge(t_v, tt + 1)
                    if tt >= 2:
                        v.wait_ge(s_vw[tt % 2], 16 * (tt // 2))
                    v.wait_ge(d_self, 3 * tt + 3)
                    v.tensor_scalar(vt1[tt % 2][:, 0:512], P0[:],
                                    rsig[:, tt % 2:tt % 2 + 1], None, ALU.mult)
                    v.tensor_scalar(vt1[tt % 2][:, 512:1024], P1[:],
                                    rsig[:, tt % 2:tt % 2 + 1], None,
                                    ALU.mult).then_inc(d_vev, 1)
                if t < NT:
                    v.wait_ge(a_sq, 2 * t + 2)
                    v.tensor_scalar(negmu[:, t % 2:t % 2 + 1],
                                    nsum[:, t % 2:t % 2 + 1], -1.0 / D,
                                    None, ALU.mult)
                    dsc[0] += 1
                    v.tensor_scalar(e2c[:], ssq[:, t % 2:t % 2 + 1], 1.0 / D,
                                    None, ALU.mult).then_inc(d_self, 1)
                    v.wait_ge(d_self, dsc[0])
                    dsc[0] += 1
                    v.scalar_tensor_tensor(nvc[:], negmu[:, t % 2:t % 2 + 1],
                                           negmu[:, t % 2:t % 2 + 1], e2c[:],
                                           ALU.mult, ALU.subtract
                                           ).then_inc(d_self, 1)
                    v.wait_ge(d_self, dsc[0])
                    v.tensor_scalar(vpe[:, t % 2:t % 2 + 1], nvc[:], -1.0, LNEPS,
                                    ALU.mult, ALU.add).then_inc(d_vpe, 1)
                    v.wait_ge(a_sqr, t + 1)
                    dsc[0] += 1
                    v.reciprocal(rsig[:, t % 2:t % 2 + 1],
                                 sigc[:, t % 2:t % 2 + 1]).then_inc(d_self, 1)
                    assert dsc[0] == 3 * t + 3
            if upto < 2:
                return
            v.wait_ge(t_sum, NT)
            v.tensor_copy(qs_row[0:1, 0:512], P4[0:1, :])
            v.tensor_copy(qs_row[0:1, 512:1024], P5[0:1, :])
            v.tensor_copy(ks_row[0:1, 0:512], P6[0:1, :])
            v.tensor_copy(ks_row[0:1, 512:1024], P7[0:1, :]).then_inc(d_rows, 1)
            pscol = bfv(P0, 32)
            v.wait_ge(t_cols, 2)
            for c in range(C):
                v.tensor_scalar(ksb[0:64, c, 0:1], pscol[0:64, 2 * c:2 * c + 1],
                                EPS, None, ALU.add)
                v.tensor_scalar(ksb[64:128, c, 1:2], pscol[64:128, 2 * c:2 * c + 1],
                                EPS, None, ALU.add)
                v.tensor_scalar(qsb[0:64, c, 0:1],
                                pscol[0:64, 16 + 2 * c:16 + 2 * c + 1], EPS,
                                None, ALU.add)
                v.tensor_scalar(qsb[64:128, c, 1:2],
                                pscol[64:128, 16 + 2 * c:16 + 2 * c + 1],
                                EPS, None, ALU.add)
            v.reduce_sum(ep_tmp[0:1, :].unsqueeze(2),
                         ks_row[0:1, :].rearrange("p (h d) -> p h d", h=H),
                         axis=AX.X, opt_input=False, opt_output=False
                         ).then_inc(d_self, 1)
            dsc[0] += 1
            v.wait_ge(d_self, dsc[0])
            dsc[0] += 1
            v.tensor_scalar(ep_tmp[:], ep_tmp[:], 64.0 * EPS, EPS, ALU.add,
                            ALU.mult).then_inc(d_self, 1)
            v.wait_ge(d_self, dsc[0])
            v.tensor_copy(eks_row[:], ep_tmp[:])
            v.reduce_sum(ep_tmp[0:1, :].unsqueeze(2),
                         qs_row[0:1, :].rearrange("p (h d) -> p h d", h=H),
                         axis=AX.X, opt_input=False, opt_output=False
                         ).then_inc(d_self, 1)
            dsc[0] += 1
            v.wait_ge(d_self, dsc[0])
            dsc[0] += 1
            v.tensor_scalar(ep_tmp[:], ep_tmp[:], 64.0 * EPS, EPS, ALU.add,
                            ALU.mult).then_inc(d_self, 1)
            v.wait_ge(d_self, dsc[0])
            v.tensor_copy(eqs_row[:], ep_tmp[:]).then_inc(d_blk, 1)
            psA16 = [P0[:, 0:16], P1[:, 0:16]]
            psB16 = [P2[:, 0:16], P3[:, 0:16]]
            for t in range(NT):
                v.wait_ge(t_dA, t + 1)
                v.reciprocal(si_all[:, t, :], psA16[t % 2])
                v.wait_ge(t_dB, t + 1)
                dsc[0] += 1
                v.reciprocal(so_t[:, t % 2, :], psB16[t % 2]
                             ).then_inc(d_self, 1)
                v.wait_ge(d_self, dsc[0])
                v.engine_nop().then_inc(d_rec, 2)
                v.wait_ge(s_q2[t % 2], 16 * (t // 2 + 1))
                v.wait_ge(s_k2[t % 2], 16 * (t // 2 + 1))
                if t >= 1:
                    v.wait_ge(t_sum2, t)
                v.tensor_tensor(
                    qsi_t[0][:, :].rearrange("p (h d) -> p h d", h=H),
                    qst[t % 2][:, :].rearrange("p (h d) -> p h d", h=H),
                    si_all[:, t, :].unsqueeze(2).broadcast_to([128, H, HD]),
                    ALU.mult)
                v.tensor_tensor(
                    kso_t[0][:, :].rearrange("p (h d) -> p h d", h=H),
                    kst2[t % 2][:, :].rearrange("p (h d) -> p h d", h=H),
                    so_t[:, t % 2, :].unsqueeze(2).broadcast_to([128, H, HD]),
                    ALU.mult).then_inc(d_p2, 1)
            v.wait_ge(t_sum2, NT)
            v.tensor_copy(sq_row[0:1, 0:512], P4[0:1, :])
            v.tensor_copy(sq_row[0:1, 512:1024], P5[0:1, :])
            v.tensor_copy(sk_row[0:1, 0:512], P6[0:1, :])
            v.tensor_copy(sk_row[0:1, 512:1024], P7[0:1, :]).then_inc(d_rows, 1)
            v.wait_ge(t_cols, 3)
            for c in range(C):
                v.tensor_scalar(skb[0:64, c, 0:1], pscol[0:64, 2 * c:2 * c + 1],
                                EPS, None, ALU.add)
                v.tensor_scalar(skb[64:128, c, 1:2], pscol[64:128, 2 * c:2 * c + 1],
                                EPS, None, ALU.add)
                v.tensor_scalar(sqb16[0:64, c, 2 * c:2 * c + 1],
                                pscol[0:64, 16 + 2 * c:16 + 2 * c + 1], EPS,
                                None, ALU.add)
                v.tensor_scalar(sqb16[64:128, c, 2 * c + 1:2 * c + 2],
                                pscol[64:128, 16 + 2 * c:16 + 2 * c + 1],
                                EPS, None, ALU.add)
            v.reduce_sum(ep_tmp[0:1, :].unsqueeze(2),
                         sk_row[0:1, :].rearrange("p (h d) -> p h d", h=H),
                         axis=AX.X, opt_input=False, opt_output=False
                         ).then_inc(d_self, 1)
            dsc[0] += 1
            v.wait_ge(d_self, dsc[0])
            dsc[0] += 1
            v.tensor_scalar(ep_tmp[:], ep_tmp[:], 64.0 * EPS, EPS, ALU.add,
                            ALU.mult).then_inc(d_self, 1)
            v.wait_ge(d_self, dsc[0])
            v.tensor_copy(esk_row[:], ep_tmp[:])
            v.reduce_sum(ep_tmp[0:1, :].unsqueeze(2),
                         sq_row[0:1, :].rearrange("p (h d) -> p h d", h=H),
                         axis=AX.X, opt_input=False, opt_output=False
                         ).then_inc(d_self, 1)
            dsc[0] += 1
            v.wait_ge(d_self, dsc[0])
            dsc[0] += 1
            v.tensor_scalar(ep_tmp[:], ep_tmp[:], 64.0 * EPS, EPS, ALU.add,
                            ALU.mult).then_inc(d_self, 1)
            v.wait_ge(d_self, dsc[0])
            v.tensor_copy(esq_row[:], ep_tmp[:]).then_inc(d_blk, 1)
            if upto < 3:
                return
            for gi in range(NG):
                v.wait_ge(t_C, gi + 1)
                pc = [P2[0:16, :], P3[0:16, :]][gi % 2]
                v.tensor_scalar(csrc_fm[:, gi * 512:(gi + 1) * 512], pc,
                                1.0, -1.0, ALU.min, ALU.max).then_inc(d_C, 1)
                for t4 in range(4):
                    t = gi * 4 + t4
                    v.wait_ge(a_sa, t + 1)
                    v.tensor_tensor(sia_all[:, t, :], si_all[:, t, :],
                                    sa_t[:, t % 2, :], ALU.mult).then_inc(d_sia, 1)
            v.wait_ge(d_C, NG)
            v.reduce_max(mx16[:], csrc_fm[:], axis=AX.X, opt_input=False,
                         opt_output=False).then_inc(d_self, 1)
            dsc[0] += 1
            v.wait_ge(d_self, dsc[0])
            v.tensor_scalar(nmx16[:], mx16[:], -1.0, None, ALU.mult).then_inc(d_mx, 1)
            v.wait_ge(a_exp, 1)
            dsc[0] += 1
            v.reciprocal(rse16[:], se16[:]).then_inc(d_self, 1)
            v.wait_ge(d_self, dsc[0])
            v.tensor_scalar(exp_fm[:], exp_fm[:], rse16[:], SFL,
                            ALU.mult, ALU.mult).then_inc(d_scf, 1)
            psT = [bfv(P4, 16), bfv(P5, 16)]
            for i in range(NT):
                v.wait_ge(t_scT, i + 1)
                v.tensor_copy(sc_tm[:, i, :], psT[i % 2][:, :]).then_inc(d_sct, 1)
            if upto < 4:
                return
            for t in range(NT):
                v.wait_ge(s_v4[t % 3], 16 * (t // 3 + 1))
                if t >= 2:
                    v.wait_ge(t_kv4, t - 1)
                v.tensor_tensor(
                    vsc_t[t % 2][:, :].rearrange("p (h d) -> p h d", h=H),
                    vst[t % 3][:, :].rearrange("p (h d) -> p h d", h=H),
                    sc_tm[:, t, :].unsqueeze(2).broadcast_to([128, H, HD]),
                    ALU.mult).then_inc(d_vsc, 1)
            v.wait_ge(t_kv4, NT)
            for c in range(C):
                ps = [P2, P3][c // 4]
                off = (c % 4) * 128
                v.tensor_copy(kv2[0:64, c, 0:64], ps[0:64, off:off + 64])
                if c == C - 1:
                    v.tensor_copy(kv2[64:128, c, 64:128],
                                  ps[64:128, off + 64:off + 128]).then_inc(d_kv2, 1)
                else:
                    v.tensor_copy(kv2[64:128, c, 64:128],
                                  ps[64:128, off + 64:off + 128])
            if upto < 5:
                return
            for gi in range(NG):
                for t4 in range(4):
                    t = gi * 4 + t4
                    v.wait_ge(s_q5[t % 4], 16 * (t // 4 + 1))
                    if t >= 2:
                        v.wait_ge(t_qxT, t - 1)
                    v.tensor_tensor(
                        qx_t[t % 2][:, :].rearrange("p (h d) -> p h d", h=H),
                        qst5[t % 4][:, :].rearrange("p (h d) -> p h d", h=H),
                        sia_all[:, t, :].unsqueeze(2).broadcast_to([128, H, HD]),
                        ALU.mult).then_inc(d_qx, 1)
                for c in range(1, C, 2):
                    v.wait_ge(t_ko, gi * C + c + 1)
                    v.tensor_copy(qxfm[gi % 3][:, c, :], [P0, P1][c % 2][:]
                                  ).then_inc(e5d, 1)
                for t4 in range(4):
                    t = gi * 4 + t4
                    v.wait_ge(t_wo, t + 1)
                    v.wait_ge(s_x5[t % 3], 16 * (t // 3 + 1))
                    if t >= 2:
                        v.wait_ge(s_out1[t % 2], 16 * (t // 2))
                        v.wait_ge(a_sq2, 2 * (t - 1))
                    pw0, pw1 = [(P4, P5), (P6, P7)][t % 2]
                    v.scalar_tensor_tensor(x1t[t % 2][:, 0:512], pw0[:], 1.0,
                                           xt5[t % 3][:, 0:512], ALU.mult, ALU.add)
                    v.scalar_tensor_tensor(x1t[t % 2][:, 512:1024], pw1[:], 1.0,
                                           xt5[t % 3][:, 512:1024], ALU.mult,
                                           ALU.add).then_inc(d_x1, 1)
                    v.wait_ge(a_sq2, 2 * t + 2)
                    v.tensor_scalar(negmu[:, t % 2:t % 2 + 1],
                                    nsum[:, t % 2:t % 2 + 1], -1.0 / D,
                                    None, ALU.mult)
                    dsc[0] += 1
                    v.tensor_scalar(e2c[:], ssq[:, t % 2:t % 2 + 1], 1.0 / D,
                                    None, ALU.mult).then_inc(d_self, 1)
                    v.wait_ge(d_self, dsc[0])
                    dsc[0] += 1
                    v.scalar_tensor_tensor(nvc[:], negmu[:, t % 2:t % 2 + 1],
                                           negmu[:, t % 2:t % 2 + 1], e2c[:],
                                           ALU.mult, ALU.subtract
                                           ).then_inc(d_self, 1)
                    v.wait_ge(d_self, dsc[0])
                    v.tensor_scalar(vpe[:, t % 2:t % 2 + 1], nvc[:], -1.0, LNEPS,
                                    ALU.mult, ALU.add).then_inc(d_vp2, 1)
                    if t >= 1:
                        tt = t - 1
                        v.wait_ge(a_sr2, tt + 1)
                        dsc[0] += 1
                        v.reciprocal(rsig[:, tt % 2:tt % 2 + 1],
                                     sigc[:, tt % 2:tt % 2 + 1]).then_inc(d_self, 1)
                        v.wait_ge(d_self, dsc[0])
                        if tt >= 2:
                            v.wait_ge(s_ln2w[tt % 2], 16 * (tt // 2))
                        v.tensor_scalar(ln2t[tt % 2][:], x1t[tt % 2][:],
                                        negmu[:, tt % 2:tt % 2 + 1],
                                        rsig[:, tt % 2:tt % 2 + 1],
                                        ALU.add, ALU.mult).then_inc(d_ln2, 1)
                if gi == NG - 1:
                    tt = NT - 1
                    v.wait_ge(a_sr2, tt + 1)
                    dsc[0] += 1
                    v.reciprocal(rsig[:, tt % 2:tt % 2 + 1],
                                 sigc[:, tt % 2:tt % 2 + 1]).then_inc(d_self, 1)
                    v.wait_ge(d_self, dsc[0])
                    v.wait_ge(s_ln2w[tt % 2], 16 * (tt // 2))
                    v.tensor_scalar(ln2t[tt % 2][:], x1t[tt % 2][:],
                                    negmu[:, tt % 2:tt % 2 + 1],
                                    rsig[:, tt % 2:tt % 2 + 1],
                                    ALU.add, ALU.mult).then_inc(d_ln2, 1)
            if upto < 6:
                return
            for gi in range(NG):
                for hc in range(HC):
                    idx = gi * HC + hc
                    v.wait_ge(a_sg, idx + 1)
                    v.wait_ge(t_U, idx + 1)
                    if hc == 0 and gi >= 1:
                        v.wait_ge(t_dn, 4 * gi)
                    v.tensor_tensor(h_fm[:, hc, :], sg_t[hc % 2][:],
                                    [P2, P3][hc % 2][:], ALU.mult).then_inc(d_h, 1)
                for t4 in range(4):
                    gt = gi * 4 + t4
                    v.wait_ge(t_dn, gt + 1)
                    if gt >= 2:
                        v.wait_ge(s_out2[gt % 2], 16 * (gt // 2))
                    pd0, pd1 = [(P4, P5), (P6, P7)][t4 % 2]
                    v.tensor_scalar(fout[t4 % 2][:, 0:512], pd0[:], DSC, None,
                                    ALU.mult)
                    v.tensor_scalar(fout[t4 % 2][:, 512:1024], pd1[:], DSC, None,
                                    ALU.mult).then_inc(d_fo, 1)

    return nc


def kernel(L0_attn_x, L0_attn_Wq, L0_attn_bq, L0_attn_Wk, L0_attn_bk,
           L0_attn_Wv, L0_attn_bv, L0_attn_Wo, L0_attn_bo,
           L0_ffn_w_gate, L0_ffn_w_up, L0_ffn_w_down):
    x = np.ascontiguousarray(np.asarray(L0_attn_x, dtype=np.float32))
    wqT = np.ascontiguousarray(np.asarray(L0_attn_Wq, np.float32).T)
    wkT = np.ascontiguousarray(np.asarray(L0_attn_Wk, np.float32).T)
    wvT = np.ascontiguousarray(np.asarray(L0_attn_Wv, np.float32).T)
    woT = np.ascontiguousarray(np.asarray(L0_attn_Wo, np.float32).T)
    bcat = np.ascontiguousarray(np.concatenate([
        np.asarray(L0_attn_bq, np.float32), np.asarray(L0_attn_bk, np.float32),
        np.asarray(L0_attn_bv, np.float32), np.asarray(L0_attn_bo, np.float32),
    ])[None, :])
    wgT = np.ascontiguousarray(np.asarray(L0_ffn_w_gate, np.int32).T)
    wuT = np.ascontiguousarray(np.asarray(L0_ffn_w_up, np.int32).T)
    wdT = np.ascontiguousarray(np.asarray(L0_ffn_w_down, np.int32).T)

    nc = build_program()
    in_maps = []
    for c in range(8):
        in_maps.append(dict(x=np.ascontiguousarray(x[c]), wqT=wqT, wkT=wkT,
                            wvT=wvT, woT=woT, bcat=bcat, wgT=wgT, wuT=wuT,
                            wdT=wdT))
    res = run_bass_kernel_spmd(nc, in_maps, core_ids=list(range(8)))
    out = np.stack([np.asarray(r["out"]) for r in res.results], axis=0)
    return out.astype(np.float32)
